# version 01
# baseline (speedup 1.0000x reference)
import contextlib
import numpy as np

import concourse.bass as bass
import concourse.mybir as mybir
from concourse.bass_utils import run_bass_kernel_spmd

F32 = mybir.dt.float32
BF16 = mybir.dt.bfloat16
U32 = mybir.dt.uint32
AF = mybir.ActivationFunctionType
ALU = mybir.AluOpType
AX = mybir.AxisListType.X

B, T, C = 8, 2048, 1024
NCH = C // 128
NH = 16
NCORES = 8
NEG = -1.0e30

_UNIQ = [0]


def _uq(base):
    _UNIQ[0] += 1
    return f"{base}{_UNIQ[0]}"


class Sem:
    def __init__(self, h):
        self.h = h
        self.n = 0

    def dma(self, ins):
        ins.then_inc(self.h, 16)
        self.n += 16
        return self.n

    def inc(self, ins):
        ins.then_inc(self.h, 1)
        self.n += 1
        return self.n


class Ctx:
    def __init__(self, nc, blk, sems):
        self.nc = nc
        self.blk = blk
        self.s = sems

    def barrier(self):
        tgts = [(x.h, x.n) for x in self.s.values() if x.n]

        for deco in (self.blk.sync, self.blk.tensor, self.blk.vector,
                     self.blk.scalar, self.blk.gpsimd):
            @deco
            def _(eng, tgts=tgts):
                for h, n in tgts:
                    eng.wait_ge(h, n)


def emit_layernorm(cx, xt, nt, sc):
    nc, blk, s = cx.nc, cx.blk, cx.s
    PE, VE, SE = s["PE"], s["VE"], s["SE"]
    ones_col_f = sc["ones_col_f"]
    ones_row_f = sc["ones_row_f"]
    pf = sc["poolf"]
    sq0, sq1 = pf[:, 0:512], pf[:, 512:1024]
    rowA, rowB = pf[0:1, 1024:1536], pf[0:1, 1536:2048]
    rowC = pf[0:1, 0:512]
    stps = contextlib.ExitStack()
    ps_sum = stps.enter_context(nc.psum_tensor(_uq("ln_ps_sum"), [1, 512], F32))
    ps_ssq = stps.enter_context(nc.psum_tensor(_uq("ln_ps_ssq"), [1, 512], F32))
    ps_ab = stps.enter_context(nc.psum_tensor(_uq("ln_ps_ab"), [128, 1024], F32))

    for tp in range(4):
        t0 = tp * 512
        pe_base = PE.n
        sq_marks = []

        @blk.scalar
        def _(se, t0=t0, pe_base=pe_base, sq_marks=sq_marks):
            for c in range(NCH):
                if c >= 2:
                    se.wait_ge(PE.h, pe_base + c - 1)
                ins = se.activation(sq0 if c % 2 == 0 else sq1,
                                    xt[:, c * T + t0: c * T + t0 + 512],
                                    AF.Square)
                sq_marks.append(SE.inc(ins))

        @blk.tensor
        def _(te, t0=t0, sq_marks=sq_marks):
            for c in range(NCH):
                te.wait_ge(SE.h, sq_marks[c])
                nc.tensor.matmul(ps_sum[:], ones_col_f[:],
                                 xt[:, c * T + t0: c * T + t0 + 512],
                                 start=(c == 0), stop=(c == NCH - 1))
                ins = nc.tensor.matmul(ps_ssq[:], ones_col_f[:],
                                       sq0 if c % 2 == 0 else sq1,
                                       start=(c == 0), stop=(c == NCH - 1))
                PE.inc(ins)

        @blk.vector
        def _(ve):
            ve.wait_ge(PE.h, PE.n)
            ve.tensor_scalar_mul(rowA, ps_sum[:], 1.0 / C)
            VE.inc(ve.tensor_scalar_mul(rowB, ps_ssq[:], 1.0 / C))
            ve.wait_ge(VE.h, VE.n)
            VE.inc(ve.tensor_mul(rowC, rowA, rowA))
            ve.wait_ge(VE.h, VE.n)
            VE.inc(ve.scalar_tensor_tensor(rowB, rowC, -1.0, rowB,
                                           ALU.mult, ALU.add))
            ve.wait_ge(VE.h, VE.n)
            ins = ve.tensor_scalar_add(rowB, rowB, 1e-5)
            VE.inc(ins)

        @blk.scalar
        def _(se):
            se.wait_ge(VE.h, VE.n)
            ins = se.activation(rowC, rowB, AF.Sqrt)
            SE.inc(ins)

        @blk.vector
        def _(ve):
            ve.wait_ge(SE.h, SE.n)
            VE.inc(ve.reciprocal(rowB, rowC))
            ve.wait_ge(VE.h, VE.n)
            ins = ve.scalar_tensor_tensor(rowA, rowA, -1.0, rowB,
                                          ALU.mult, ALU.mult)
            VE.inc(ins)

        @blk.tensor
        def _(te):
            te.wait_ge(VE.h, VE.n)
            nc.tensor.matmul(ps_ab[:, 0:512], ones_row_f[:], rowB,
                             start=True, stop=True)
            ins = nc.tensor.matmul(ps_ab[:, 512:1024], ones_row_f[:], rowA,
                                   start=True, stop=True)
            PE.inc(ins)

        @blk.vector
        def _(ve, t0=t0):
            ve.wait_ge(PE.h, PE.n)
            tmps = [pf[:, 0:512], pf[:, 512:1024], pf[:, 1024:1536],
                    pf[:, 1536:2048]]
            for cg in range(0, NCH, 4):
                ins = None
                for j in range(4):
                    c = cg + j
                    ins = ve.tensor_mul(tmps[j],
                                        xt[:, c * T + t0: c * T + t0 + 512],
                                        ps_ab[:, 0:512])
                VE.inc(ins)
                ve.wait_ge(VE.h, VE.n)
                for j in range(4):
                    c = cg + j
                    ins = ve.tensor_add(nt[:, c * T + t0: c * T + t0 + 512],
                                        tmps[j], ps_ab[:, 512:1024])
                VE.inc(ins)
                ve.wait_ge(VE.h, VE.n)

        cx.barrier()

    stps.close()


def mm_stream(cx, sc, *, mch, kch, lhsT_fn, rhs_fn, evac, wload=None,
              wkw=None, npieces=4, tw=512, bias_row_fn=None, mrows=128):
    nc, blk, s = cx.nc, cx.blk, cx.s
    GD, PE, EV = s["GD"], s["PE"], s["EV"]
    stps = contextlib.ExitStack()
    ps = stps.enter_context(nc.psum_tensor(_uq("mm_ps"), [128, 2 * tw], F32))
    wbf = sc.get("w_bf")
    ones_row_bf = sc["ones_row_bf"]

    pe_base, ev_base = PE.n, EV.n
    groups = [(mc, tp) for mc in range(mch) for tp in range(npieces)]
    pe_marks = {}
    wsems = (s["GD0"], s["GD1"])
    wmarks = {}

    if wload is not None:
        @blk.gpsimd
        def _(gp):
            for mc in range(mch):
                if mc >= 2:
                    gp.wait_ge(PE.h, pe_base + (mc - 1) * npieces)
                src = wload(mc)
                dst = wbf[:, (mc % 2) * wkw:((mc % 2) + 1) * wkw]
                ins = gp.dma_start(
                    out=dst.rearrange("p (k m) -> p k m", k=kch),
                    in_=src.rearrange("(k p) m -> p k m", p=128))
                wmarks[mc] = wsems[mc % 2].dma(ins)

    @blk.tensor
    def _(te):
        for g, (mc, tp) in enumerate(groups):
            if wload is not None and tp == 0:
                te.wait_ge(wsems[mc % 2].h, wmarks[mc])
            if g >= 2:
                te.wait_ge(EV.h, ev_base + g - 1)
            elif g == 0 and ev_base:
                te.wait_ge(EV.h, ev_base)
            slot = g % 2
            psap = ps[0:mrows, slot * tw:slot * tw + tw]
            ops = [(lhsT_fn(mc, kc, mc % 2), rhs_fn(kc, tp))
                   for kc in range(kch)]
            if bias_row_fn is not None:
                ops.append((bias_row_fn(mc), ones_row_bf[:, 0:tw]))
            ins = None
            for i, (l, r) in enumerate(ops):
                ins = nc.tensor.matmul(psap, l, r, start=(i == 0),
                                       stop=(i == len(ops) - 1))
            pe_marks[g] = PE.inc(ins)

    ename, efn = evac
    deco = getattr(blk, ename)

    @deco
    def _(eng):
        for g, (mc, tp) in enumerate(groups):
            eng.wait_ge(PE.h, pe_marks[g])
            slot = g % 2
            psap = ps[0:mrows, slot * tw:slot * tw + tw]
            ins = efn(eng, mc, tp, psap)
            EV.inc(ins)

    stps.close()


def emit_l0attn(cx, bufs, aps, sc, phase="all"):
    nc, blk, s = cx.nc, cx.blk, cx.s
    GD, PE, VE, SE = s["GD"], s["PE"], s["VE"], s["SE"]
    xt, nt = bufs["xt"], bufs["nt"]
    qp, kp = bufs["qp"], bufs["kp"]
    bqc_bf, bqc_f = sc["bqc_bf"], sc["bqc_f"]

    emit_layernorm(cx, xt, nt, sc)
    if phase == "ln":
        return

    wqk2 = sc["wqk2"]
    projb = sc["projb"]
    fwqk = sc["fwqk"]
    brow = sc["brow"]

    with contextlib.ExitStack() as st:
        fold_ps = st.enter_context(nc.psum_tensor("fold_ps", [128, 1024], F32))
        brow_ps = st.enter_context(nc.psum_tensor("brow_ps", [1, 256], F32))

        for pair in range(8):
            h0, h1 = 2 * pair, 2 * pair + 1

            @blk.gpsimd
            def _(gp, h0=h0, h1=h1):
                gp.wait_ge(PE.h, PE.n)
                GD.dma(gp.dma_start(out=projb[0:64, 0:64], in_=aps["proj"][h0]))
                GD.dma(gp.dma_start(out=projb[64:128, 64:128],
                                    in_=aps["proj"][h1]))

            for qk in range(2):
                qoff = qk * 1024

                @blk.gpsimd
                def _(gp, h0=h0, h1=h1, qoff=qoff, qk=qk):
                    if qk == 1:
                        gp.wait_ge(PE.h, PE.n)
                    GD.dma(gp.dma_start(
                        out=wqk2[0:64, :],
                        in_=aps["wqkv_r"][qoff + h0 * 64:qoff + (h0 + 1) * 64, :]))
                    GD.dma(gp.dma_start(
                        out=wqk2[64:128, :],
                        in_=aps["wqkv_r"][qoff + h1 * 64:qoff + (h1 + 1) * 64, :]))

                @blk.tensor
                def _(te, pair=pair, qk=qk):
                    te.wait_ge(GD.h, GD.n)
                    te.wait_ge(SE.h, SE.n)
                    ins = None
                    for hh in range(2):
                        for cc in range(NCH):
                            ins = nc.tensor.matmul(
                                fold_ps[:, cc * 128 + hh * 64:cc * 128 + hh * 64 + 64],
                                wqk2[hh * 64:(hh + 1) * 64, cc * 128:(cc + 1) * 128],
                                projb[hh * 64:(hh + 1) * 64, hh * 64:hh * 64 + 64],
                                start=True, stop=True)
                        hcol = 2 * pair + hh
                        bc = bqc_bf[(hcol % 2) * 64:(hcol % 2) * 64 + 64,
                                    qk * 8 + hcol // 2: qk * 8 + hcol // 2 + 1]
                        ins = nc.tensor.matmul(
                            brow_ps[:, qk * 128 + hh * 64:qk * 128 + hh * 64 + 64],
                            bc, projb[hh * 64:(hh + 1) * 64, hh * 64:hh * 64 + 64],
                            start=True, stop=True)
                    PE.inc(ins)

                @blk.scalar
                def _(se, qk=qk):
                    se.wait_ge(PE.h, PE.n)
                    se.copy(fwqk[:], fold_ps[:])
                    ins = se.copy(brow[:, qk * 128:(qk + 1) * 128],
                                  brow_ps[:, qk * 128:(qk + 1) * 128])
                    SE.inc(ins)

                import os
                sub = os.environ.get("L1_QK_SUB", "all")
                if sub == "fold":
                    continue
                dstbuf = qp if qk == 0 else kp

                def evac_relu(eng, mc, tp, psap, pair=pair, dstbuf=dstbuf):
                    return eng.activation(
                        dstbuf[:, pair * T + tp * 512: pair * T + tp * 512 + 512],
                        psap, AF.Relu)

                semark = SE.n

                @blk.tensor
                def _(te, semark=semark):
                    te.wait_ge(SE.h, semark)

                mm_stream(
                    cx, sc, mch=1, kch=NCH,
                    lhsT_fn=(lambda mc, kc, slot:
                             fwqk[:, kc * 128:(kc + 1) * 128]),
                    rhs_fn=lambda kc, tp: nt[:, kc * T + tp * 512: kc * T + tp * 512 + 512],
                    bias_row_fn=(None if sub == "nobias" else
                                 (lambda mc, qk=qk: brow[:, qk * 128:(qk + 1) * 128])),
                    evac=("scalar", evac_relu))

        cx.barrier()

    if phase == "qk":
        return
    eye_bf = sc["eye_bf"]
    triu_f = sc["triu_f"]
    ones_col_bf = sc["ones_col_bf"]
    atm_sb = sc["atm_sb"]
    ktv_sb = sc["ktv_sb"]
    s_f = sc["s_f"]
    ssb = sc["ssb"]
    ctx_sb = sc["ctx_sb"]
    r_sb = sc["r_sb"]
    r_bf = sc["r_bf"]
    v_sb = sc["v_sb"]
    wv_bf = sc["w_bf"]

    with contextlib.ExitStack() as st:
        at2 = [st.enter_context(nc.psum_tensor(f"at2_{i}", [128, 512], F32))
               for i in range(2)]
        tr2s = st.enter_context(nc.psum_tensor("tr2s", [128, 128], BF16))
        tr2 = [tr2s, tr2s]
        dsz = [st.enter_context(nc.psum_tensor(f"dsz_{i}", [128, 128], F32))
               for i in range(2)]
        nrm = [st.enter_context(nc.psum_tensor(f"nrm_{i}", [1, 128], F32))
               for i in range(2)]
        v_ps = st.enter_context(nc.psum_tensor("v_ps", [128, 128], F32))

        wvt = sc["wvt_bf"]

        @blk.gpsimd
        def _(gp):
            ins = gp.dma_start(
                out=wvt[:].rearrange("p (k m) -> p k m", k=NCH),
                in_=aps["wvT"][:, :].rearrange("(k p) m -> p k m", p=128))
            GD.dma(ins)

        cx.barrier()

        pe_at, pe_tr, pe_mid, pe_rb = {}, {}, {}, {}
        pe_vmm, se_vev = {}, {}
        se_ktv, se_ctx = {}, {}
        ve_atm, ve_r, ve_out = {}, {}, {}

        for c in range(16):
            vslot = c % 2
            for mc in range(NCH):
                @blk.tensor
                def _(te, c=c, mc=mc):
                    if c * 8 + mc > 0:
                        te.wait_ge(SE.h, se_vev[c * 8 + mc - 1])
                    ins = None
                    for kc in range(NCH):
                        ins = nc.tensor.matmul(
                            v_ps[:],
                            wvt[:, kc * 1024 + mc * 128: kc * 1024 + (mc + 1) * 128],
                            nt[:, kc * T + c * 128: kc * T + (c + 1) * 128],
                            start=(kc == 0), stop=(kc == NCH - 1))
                    pe_vmm[c * 8 + mc] = PE.inc(ins)

                @blk.scalar
                def _(se, c=c, mc=mc, vslot=vslot):
                    se.wait_ge(PE.h, pe_vmm[c * 8 + mc])
                    if c >= 2 and mc == 0:
                        se.wait_ge(PE.h, pe_tr[(c - 2) * NH + NH - 1])
                    ins = se.activation(
                        v_sb[:, vslot * 1024 + mc * 128: vslot * 1024 + (mc + 1) * 128],
                        v_ps[:], AF.Identity, bias=bqc_f[:, 16 + mc:17 + mc])
                    se_vev[c * 8 + mc] = SE.inc(ins)

            for h in range(NH):
                it = c * NH + h
                pp = it % 2
                ch = h // 2
                hb = (h % 2) * 64
                qsl = qp[hb:hb + 64, ch * T + c * 128: ch * T + (c + 1) * 128]
                ksl = kp[hb:hb + 64, ch * T + c * 128: ch * T + (c + 1) * 128]
                vsl = v_sb[hb:hb + 64, vslot * 1024 + ch * 128:vslot * 1024 + (ch + 1) * 128]

                @blk.tensor
                def _(te, it=it, pp=pp, c=c, h=h, hb=hb, qsl=qsl, ksl=ksl,
                      vsl=vsl):
                    if it >= 2:
                        te.wait_ge(VE.h, ve_out[it - 2])
                    if h == 0:
                        te.wait_ge(SE.h, se_vev[c * 8 + NCH - 1])
                    g = h // 2
                    ins = nc.tensor.matmul(at2[pp][:, 0:128], ksl, qsl,
                                           start=True, stop=True)
                    pe_at[it] = PE.inc(ins)

                @blk.vector
                def _(ve, it=it, pp=pp):
                    ve.wait_ge(PE.h, pe_at[it])
                    ins = ve.tensor_mul(atm_sb[:, pp * 128:(pp + 1) * 128],
                                        at2[pp][:, 0:128], triu_f[:])
                    ve_atm[it] = VE.inc(ins)

                @blk.tensor
                def _(te, it=it, pp=pp, c=c, h=h, hb=hb, qsl=qsl, ksl=ksl,
                      vsl=vsl):
                    te.wait_ge(VE.h, ve_atm[it])
                    g = h // 2
                    if c > 0:
                        nc.tensor.matmul(
                            at2[pp][hb:hb + 64, 256:384],
                            ssb[hb:hb + 64, g * 65:g * 65 + 64], qsl,
                            start=True, stop=False,
                            tile_position=(hb, hb) if hb else None)
                        nc.tensor.matmul(
                            nrm[pp][:],
                            ssb[hb:hb + 64, g * 65 + 64:g * 65 + 65], qsl,
                            start=True, stop=False,
                            tile_position=(hb, 0) if hb else None)
                    if it >= 1:
                        te.wait_ge(SE.h, se_ktv[it - 1])
                    nc.tensor.transpose(tr2[pp][:, 0:64], ksl,
                                        eye_bf[hb:hb + 64, hb:hb + 64])
                    ins = nc.tensor.transpose(tr2[pp][:, 64:128], vsl,
                                              eye_bf[hb:hb + 64, hb:hb + 64])
                    pe_tr[it] = PE.inc(ins)

                @blk.scalar
                def _(se, it=it, pp=pp):
                    se.wait_ge(PE.h, pe_tr[it])
                    ins = se.copy(ktv_sb[:, pp * 128:(pp + 1) * 128], tr2[pp][:])
                    se_ktv[it] = SE.inc(ins)

                @blk.tensor
                def _(te, it=it, pp=pp, c=c, h=h, hb=hb, qsl=qsl):
                    te.wait_ge(VE.h, ve_atm[it])
                    te.wait_ge(SE.h, se_ktv[it])
                    ktm = ktv_sb[:, pp * 128:pp * 128 + 64]
                    vtm = ktv_sb[:, pp * 128 + 64:pp * 128 + 128]
                    amsk = atm_sb[:, pp * 128:(pp + 1) * 128]
                    tp_col = (0, hb) if hb else None
                    g = h // 2
                    nc.tensor.matmul(dsz[pp][hb:hb + 64, 0:64],
                                     ktm, vtm, start=True, stop=True,
                                     tile_position=tp_col)
                    nc.tensor.matmul(dsz[pp][hb:hb + 64, 64:65],
                                     ktm, ones_col_bf[:, 0:1],
                                     start=True, stop=True,
                                     tile_position=tp_col)
                    nc.tensor.matmul(at2[pp][hb:hb + 64, 256:384], vtm, amsk,
                                     start=(c == 0), stop=True,
                                     tile_position=tp_col)
                    ins = nc.tensor.matmul(nrm[pp][:],
                                           ones_col_bf[:, 0:1], amsk,
                                           start=(c == 0), stop=True)
                    pe_mid[it] = PE.inc(ins)

                @blk.vector
                def _(ve, it=it, pp=pp, c=c, h=h, hb=hb):
                    ve.wait_ge(PE.h, pe_mid[it])
                    g = h // 2
                    sfs = s_f[hb:hb + 64, g * 65:(g + 1) * 65]
                    ve.tensor_scalar_add(r_sb[:, pp * 128:(pp + 1) * 128],
                                         nrm[pp][:], 1e-6)
                    if c == 0:
                        ins = ve.tensor_copy(sfs, dsz[pp][hb:hb + 64, 0:65])
                    else:
                        ins = ve.tensor_add(sfs, sfs, dsz[pp][hb:hb + 64, 0:65])
                    VE.inc(ins)
                    ve.wait_ge(VE.h, VE.n)
                    ve.reciprocal(r_sb[:, pp * 128:(pp + 1) * 128],
                                  r_sb[:, pp * 128:(pp + 1) * 128])
                    ins = ve.tensor_copy(ssb[hb:hb + 64, g * 65:(g + 1) * 65],
                                         sfs)
                    VE.inc(ins)
                    ve.wait_ge(VE.h, VE.n)
                    ins = ve.tensor_copy(r_bf[:, pp * 128:(pp + 1) * 128],
                                         r_sb[:, pp * 128:(pp + 1) * 128])
                    ve_r[it] = VE.inc(ins)

                @blk.tensor
                def _(te, it=it, pp=pp, hb=hb):
                    te.wait_ge(VE.h, ve_r[it])
                    ins = nc.tensor.matmul(at2[pp][hb:hb + 64, 384:512],
                                           sc["ones_row_bf64"][:],
                                           r_bf[:, pp * 128:(pp + 1) * 128],
                                           start=True, stop=True,
                                           tile_position=(0, hb) if hb else None)
                    pe_rb[it] = PE.inc(ins)

                @blk.scalar
                def _(se, it=it, pp=pp, hb=hb):
                    se.wait_ge(PE.h, pe_rb[it])
                    if it >= 2:
                        se.wait_ge(VE.h, ve_out[it - 2])
                    ins = se.copy(ctx_sb[hb:hb + 64, pp * 128:(pp + 1) * 128],
                                  at2[pp][hb:hb + 64, 256:384])
                    se_ctx[it] = SE.inc(ins)

                @blk.vector
                def _(ve, it=it, pp=pp, hb=hb, qsl=qsl):
                    ve.wait_ge(SE.h, se_ctx[it])
                    ve.wait_ge(PE.h, pe_rb[it])
                    ins = ve.tensor_mul(
                        qsl, ctx_sb[hb:hb + 64, pp * 128:(pp + 1) * 128],
                        at2[pp][hb:hb + 64, 384:512])
                    ve_out[it] = VE.inc(ins)

        cx.barrier()

    if phase == "perf":
        return

    @blk.gpsimd
    def _(gp):
        GD.dma(gp.dma_start(out=sc["bout_bf"][:], in_=aps["bout_row"]))

    cx.barrier()

    def evac_res(eng, mc, tp, psap):
        sl = xt[:, mc * T + tp * 512: mc * T + tp * 512 + 512]
        return eng.tensor_add(sl, sl, psap)

    mm_stream(cx, sc, mch=NCH, kch=NCH,
              wload=lambda mc: aps["woutT"][:, mc * 128:(mc + 1) * 128],
              wkw=NCH * 128,
              lhsT_fn=lambda mc, kc, slot: sc["w_bf"][
                  :, slot * NCH * 128 + kc * 128: slot * NCH * 128 + (kc + 1) * 128],
              rhs_fn=lambda kc, tp: qp[:, kc * T + tp * 512: kc * T + tp * 512 + 512],
              bias_row_fn=lambda mc: sc["bout_bf"][:, mc * 128:(mc + 1) * 128],
              evac=("vector", evac_res))
    cx.barrier()


def emit_gate_hist(cx, xt, nt2, aps, sc):
    nc, blk, s = cx.nc, cx.blk, cx.s
    PE, VE, SE = s["PE"], s["VE"], s["SE"]
    glog = sc["glog"]
    gtok_all = sc["gtok_all"]
    eye_bf = sc["eye_bf"]
    iota8 = sc["iota8"]
    mx8, idx8, idxf, oh8 = sc["mx8"], sc["idx8"], sc["idxf"], sc["oh8"]
    ones_col_bf = sc["ones_col_bf"]

    emit_layernorm(cx, xt, nt2, sc)

    def evac_glog(eng, mc, tp, psap):
        return eng.copy(glog[0:8, tp * 512:(tp + 1) * 512], psap)

    mm_stream(cx, sc, mch=1, kch=NCH, mrows=8,
              wload=lambda mc: aps["gwT"][:, :],
              wkw=NCH * 8,
              lhsT_fn=lambda mc, kc, slot: sc["w_bf"][
                  :, slot * NCH * 8 + kc * 8: slot * NCH * 8 + (kc + 1) * 8],
              rhs_fn=lambda kc, tp: nt2[:, kc * T + tp * 512: kc * T + tp * 512 + 512],
              evac=("scalar", evac_glog))
    cx.barrier()

    with contextlib.ExitStack() as st:
        tp_ps = st.enter_context(nc.psum_tensor("tp_ps", [128, 8], BF16))
        cnt_ps = st.enter_context(nc.psum_tensor("cnt_ps", [8, 1], F32))

        for i in range(16):
            @blk.tensor
            def _(te, i=i):
                te.wait_ge(SE.h, SE.n)
                ins = nc.tensor.transpose(tp_ps[:],
                                          glog[0:8, i * 128:(i + 1) * 128],
                                          eye_bf[0:8, 0:8])
                PE.inc(ins)

            @blk.scalar
            def _(se, i=i):
                se.wait_ge(PE.h, PE.n)
                ins = se.copy(gtok_all[:, i * 8:(i + 1) * 8], tp_ps[:])
                SE.inc(ins)

            @blk.vector
            def _(ve, i=i):
                ve.wait_ge(SE.h, SE.n)
                ve.wait_ge(PE.h, PE.n)
                gt = gtok_all[:, i * 8:(i + 1) * 8]
                VE.inc(ve.reduce_max(mx8[:, 0:1], gt, axis=AX))
                ve.wait_ge(VE.h, VE.n)
                VE.inc(ve.max_index(idx8[:], mx8[:], gt))
                ve.wait_ge(VE.h, VE.n)
                VE.inc(ve.tensor_copy(idxf[:, 0:1], idx8[:, 0:1]))
                ve.wait_ge(VE.h, VE.n)
                ins = ve.tensor_single_scalar(oh8[:], iota8[:], idxf[:, 0:1],
                                              ALU.is_equal)
                VE.inc(ins)

            @blk.tensor
            def _(te, i=i):
                te.wait_ge(VE.h, VE.n)
                ins = nc.tensor.matmul(cnt_ps[:], oh8[:], ones_col_bf[:, 0:1],
                                       start=(i == 0), stop=(i == 15))
                PE.inc(ins)

        @blk.scalar
        def _(se):
            se.wait_ge(PE.h, PE.n)
            ins = se.copy(sc["cnt_sb"][:], cnt_ps[:])
            SE.inc(ins)

        cx.barrier()


def alloc_scratch(st, nc):
    sc = {}

    def sb(name, shape, dt):
        sc[name] = st.enter_context(nc.sbuf_tensor(name, shape, dt))
        return sc[name]

    sb("poolf", [128, 2048], F32)
    sb("ones_col_f", [128, 1], F32)
    sb("ones_row_f", [1, 128], F32)
    sb("ones_row_bf", [1, 512], BF16)
    sb("ones_row_bf64", [1, 64], BF16)
    sb("ones_col_bf", [128, 1], BF16)
    sb("eye_bf", [128, 128], BF16)
    sb("triu_f", [128, 128], F32)
    sb("iota8", [128, 8], F32)
    sb("w_bf", [128, 2 * NCH * 128], BF16)
    sb("wvt_bf", [128, NCH * 1024], BF16)
    sb("bqc_f", [128, 24], F32)
    sb("bqc_bf", [128, 24], BF16)
    sb("wqk2", [128, 1024], BF16)
    sb("projb", [128, 128], BF16)
    sb("fwqk", [128, 1024], BF16)
    sb("brow", [1, 256], BF16)
    sb("atm_sb", [128, 256], BF16)
    sb("ktv_sb", [128, 256], BF16)
    sb("s_f", [128, 520], F32)
    sb("ssb", [128, 520], BF16)
    sb("ctx_sb", [128, 256], BF16)
    sb("r_sb", [1, 256], F32)
    sb("r_bf", [1, 256], BF16)
    sb("v_sb", [128, 2048], BF16)
    sb("gtok_all", [128, 128], F32)
    sb("mx8", [128, 8], F32)
    sb("idx8", [128, 8], U32)
    sb("idxf", [128, 1], F32)
    sb("oh8", [128, 8], BF16)
    sb("cnt_sb", [8, 1], F32)
    sc["glog"] = sc["poolf"][0:8, 0:1024].bitcast(BF16)
    sc["bout_bf"] = sc["poolf"][0:1, 1024:1536].bitcast(BF16)
    return sc


def emit_const_init(cx, aps, sc):
    nc, blk, s = cx.nc, cx.blk, cx.s
    LD, GD, VE = s["LD"], s["GD"], s["VE"]

    @blk.sync
    def _(q):
        LD.dma(q.dma_start(out=sc["triu_f"][:], in_=aps["c_triu"]))
        LD.dma(q.dma_start(out=sc["iota8"][:], in_=aps["c_iota8"]))
        LD.dma(q.dma_start(out=sc["bqc_f"][:], in_=aps["bqkv_cols"]))

    @blk.gpsimd
    def _(gp):
        GD.dma(gp.dma_start(out=sc["eye_bf"][:], in_=aps["c_eye"]))
        GD.dma(gp.dma_start(out=sc["bqc_bf"][:], in_=aps["bqkv_cols"]))

    @blk.vector
    def _(ve):
        ve.wait_ge(LD.h, LD.n)
        ve.wait_ge(GD.h, GD.n)
        ve.memset(sc["ones_col_f"][:], 1.0)
        ve.memset(sc["ones_row_f"][:], 1.0)
        ve.memset(sc["ones_row_bf"][:], 1.0)
        ve.memset(sc["ones_row_bf64"][:], 1.0)
        ve.memset(sc["ones_col_bf"][:], 1.0)
        ins = ve.memset(sc["mx8"][:], NEG)
        VE.inc(ins)

    cx.barrier()


def build_launch1():
    nc = bass.Bass("TRN2", target_bir_lowering=False)
    aps = {}

    def di(name, shape):
        aps[name] = nc.dram_tensor(name, shape, F32, kind="ExternalInput").ap()

    di("x_t", [C, T])
    di("wqkv_r", [3 * C, C])
    di("bqkv_cols", [128, 24])
    di("proj", [NH, 64, 64])
    di("wvT", [C, C])
    di("woutT", [C, C])
    di("bout_row", [1, C])
    di("gwT", [C, 8])
    di("c_eye", [128, 128])
    di("c_triu", [128, 128])
    di("c_iota8", [128, 8])
    x1_t = nc.dram_tensor("x1_t", [C, T], F32, kind="ExternalOutput").ap()
    cnt = nc.dram_tensor("cnt", [8, 1], F32, kind="ExternalOutput").ap()

    with contextlib.ExitStack() as st:
        ec = st.enter_context
        bufs = {
            "xt": ec(nc.sbuf_tensor("xt", [128, NCH * T], F32)),
            "nt": ec(nc.sbuf_tensor("nt", [128, NCH * T], BF16)),
            "qp": ec(nc.sbuf_tensor("qp", [128, NCH * T], BF16)),
            "kp": ec(nc.sbuf_tensor("kp", [128, NCH * T], BF16)),
        }
        sems = {k: Sem(ec(nc.semaphore(k)))
                for k in ("LD", "GD", "GD0", "GD1", "PE", "VE", "SE", "EV",
                          "ST")}
        sc = alloc_scratch(st, nc)
        blk = ec(nc.Block())
        cx = Ctx(nc, blk, sems)

        @blk.sync
        def _(q):
            ins = q.dma_start(
                out=bufs["xt"][:].rearrange("p (c t) -> p c t", c=NCH),
                in_=aps["x_t"].rearrange("(c p) t -> p c t", p=128))
            sems["LD"].dma(ins)

        import os
        phase = os.environ.get("L1_PHASE", "all")
        emit_const_init(cx, aps, sc)
        emit_l0attn(cx, bufs, aps, sc, phase=phase)
        if phase == "all":
            emit_gate_hist(cx, bufs["xt"], bufs["nt"], aps, sc)

        ST = sems["ST"]

        @blk.sync
        def _(q):
            ST.dma(q.dma_start(
                out=x1_t.rearrange("(c p) t -> p c t", p=128),
                in_=bufs["xt"][:].rearrange("p (c t) -> p c t", c=NCH)))
            ST.dma(q.dma_start(out=cnt, in_=sc["cnt_sb"][:]))
            q.wait_ge(ST.h, ST.n)

    return nc


def _cols128(v):
    v = np.ascontiguousarray(np.asarray(v, dtype=np.float32).reshape(-1))
    return np.ascontiguousarray(v.reshape(-1, 128).T)


def _consts():
    return {
        "c_eye": np.eye(128, dtype=np.float32),
        "c_triu": np.ascontiguousarray(np.triu(np.ones((128, 128), np.float32))),
        "c_iota8": np.ascontiguousarray(
            np.broadcast_to(np.arange(8, dtype=np.float32), (128, 8))),
    }


def launch1_inputs(inputs, b):
    d = {
        "x_t": np.ascontiguousarray(np.asarray(inputs["L0_attn_x"][b]).T.astype(np.float32)),
        "wqkv_r": np.ascontiguousarray(np.asarray(inputs["L0_attn_Wqkv"], np.float32)),
        "bqkv_cols": _cols128(inputs["L0_attn_bqkv"]),
        "proj": np.ascontiguousarray(np.asarray(inputs["L0_attn_proj"], np.float32)),
        "wvT": np.ascontiguousarray(np.asarray(inputs["L0_attn_Wqkv"], np.float32)[2048:3072, :].T),
        "woutT": np.ascontiguousarray(np.asarray(inputs["L0_attn_Wout"], np.float32).T),
        "bout_row": np.ascontiguousarray(np.asarray(inputs["L0_attn_bout"], np.float32)).reshape(1, C),
        "gwT": np.ascontiguousarray(np.asarray(inputs["L0_ffn_gate_weight"], np.float32).T),
    }
    d.update(_consts())
    return d





def emit_l0_gating(cx, aps, sc):
    nc, blk, s = cx.nc, cx.blk, cx.s
    LD, PE, VE, SE = s["LD"], s["PE"], s["VE"], s["SE"]
    glog = sc["glog"]
    gtok = sc["gtok_all"]
    eye_bf = sc["eye_bf"]
    iota8 = sc["iota8"]
    mx8, idx8, idxf, oh8 = sc["mx8"], sc["idx8"], sc["idxf"], sc["oh8"]
    mxB = sc["mxB"]
    idxB = sc["idxB"]
    g8 = sc["g8"]
    g8b = sc["g8b"]
    frow = sc["frow"]
    tokw = sc["tokw"]
    d0bf = sc["d0bf"]
    d0T = sc["d0T"]
    emap_b = sc["emap_b"]
    comb_sb = sc["comb_sb"]

    with contextlib.ExitStack() as st:
        p8 = st.enter_context(nc.psum_tensor(_uq("p8"), [128, 512], F32))
        p8b = st.enter_context(nc.psum_tensor(_uq("p8b"), [128, 8], BF16))
        p8c = st.enter_context(nc.psum_tensor(_uq("p8c"), [8, 128], BF16))

        @blk.sync
        def _(q):
            LD.dma(q.dma_start(out=g8[0:8, 0:8], in_=aps["counts_all"]))
            LD.dma(q.dma_start(out=g8[0:8, 8:16], in_=aps["sim"]))
            LD.dma(q.dma_start(out=sc["iotac"][:], in_=aps["c_iotac"]))

        @blk.vector
        def _(ve):
            ve.wait_ge(LD.h, LD.n)
            VE.inc(ve.reduce_max(g8[0:8, 16:17], g8[0:8, 0:8], axis=AX))
            ve.wait_ge(VE.h, VE.n)
            ins = ve.tensor_single_scalar(g8b[0:8, 0:1], g8[0:8, 16:17], 0.0,
                                          ALU.is_gt)
            VE.inc(ins)

        @blk.tensor
        def _(te):
            te.wait_ge(VE.h, VE.n)
            ins = nc.tensor.transpose(p8b[0:1, 0:8], g8b[0:8, 0:1],
                                      eye_bf[0:8, 0:8])
            PE.inc(ins)

        @blk.scalar
        def _(se):
            se.wait_ge(PE.h, PE.n)
            SE.inc(se.copy(frow[0:1, 0:8], p8b[0:1, 0:8]))

        @blk.tensor
        def _(te):
            te.wait_ge(SE.h, SE.n)
            ins = nc.tensor.matmul(p8[0:8, 0:8], sc["ones_row_bf"][:, 0:8],
                                   frow[0:1, 0:8], start=True, stop=True)
            PE.inc(ins)

        @blk.vector
        def _(ve):
            ve.wait_ge(PE.h, PE.n)
            VE.inc(ve.tensor_scalar(g8[0:8, 17:25], p8[0:8, 0:8], 1.0e30,
                                    -1.0e30, ALU.mult, ALU.add))
            ve.wait_ge(VE.h, VE.n)
            VE.inc(ve.tensor_add(g8[0:8, 25:33], g8[0:8, 8:16],
                                 g8[0:8, 17:25]))
            ve.wait_ge(VE.h, VE.n)
            VE.inc(ve.memset(mx8[0:8, :], NEG))
            ve.wait_ge(VE.h, VE.n)
            VE.inc(ve.reduce_max(mx8[0:8, 0:1], g8[0:8, 25:33], axis=AX))
            ve.wait_ge(VE.h, VE.n)
            VE.inc(ve.max_index(idx8[0:8, :], mx8[0:8, :], g8[0:8, 25:33]))
            ve.wait_ge(VE.h, VE.n)
            ve.tensor_copy(g8[0:8, 33:34], idx8[0:8, 0:1])
            VE.inc(ve.tensor_single_scalar(g8[0:8, 34:35], mx8[0:8, 0:1],
                                           0.7, ALU.is_ge))
            ve.wait_ge(VE.h, VE.n)
            VE.inc(ve.tensor_single_scalar(g8[0:8, 35:36], g8[0:8, 16:17],
                                           0.0, ALU.is_gt))
            ve.wait_ge(VE.h, VE.n)
            VE.inc(ve.tensor_scalar(g8[0:8, 36:37], g8[0:8, 35:36], -1.0,
                                    1.0, ALU.mult, ALU.add))
            ve.wait_ge(VE.h, VE.n)
            VE.inc(ve.tensor_mul(g8[0:8, 37:38], g8[0:8, 34:35],
                                 g8[0:8, 36:37]))
            ve.wait_ge(VE.h, VE.n)
            VE.inc(ve.tensor_sub(g8[0:8, 38:39], g8[0:8, 33:34],
                                 sc["iotac"][0:8, 0:1]))
            ve.wait_ge(VE.h, VE.n)
            VE.inc(ve.tensor_mul(g8[0:8, 39:40], g8[0:8, 37:38],
                                 g8[0:8, 38:39]))
            ve.wait_ge(VE.h, VE.n)
            VE.inc(ve.tensor_add(g8[0:8, 40:41], g8[0:8, 39:40],
                                 sc["iotac"][0:8, 0:1]))
            ve.wait_ge(VE.h, VE.n)
            VE.inc(ve.tensor_single_scalar(g8b[0:8, 8:16], iota8[0:8, :],
                                           g8[0:8, 40:41], ALU.is_equal))
            ve.wait_ge(VE.h, VE.n)
            VE.inc(ve.tensor_copy(g8b[0:8, 1:2], g8[0:8, 40:41]))
            ve.wait_ge(VE.h, VE.n)
            ins = ve.memset(mx8[:], NEG)
            VE.inc(ins)

        @blk.tensor
        def _(te):
            te.wait_ge(VE.h, VE.n)
            ins = nc.tensor.transpose(p8b[0:1, 0:8], g8b[0:8, 1:2],
                                      eye_bf[0:8, 0:8])
            PE.inc(ins)

        @blk.scalar
        def _(se):
            se.wait_ge(PE.h, PE.n)
            SE.inc(se.copy(frow[0:1, 8:16], p8b[0:1, 0:8]))

        @blk.tensor
        def _(te):
            te.wait_ge(SE.h, SE.n)
            ins = nc.tensor.matmul(p8[:, 8:16], sc["ones_row_bf"][:, 0:128],
                                   frow[0:1, 8:16], start=True, stop=True)
            PE.inc(ins)

        @blk.scalar
        def _(se):
            se.wait_ge(PE.h, PE.n)
            SE.inc(se.copy(emap_b[:], p8[:, 8:16]))

        cx.barrier()

        for i in range(16):
            @blk.tensor
            def _(te, i=i):
                te.wait_ge(SE.h, SE.n)
                te.wait_ge(VE.h, VE.n)
                ins = nc.tensor.transpose(p8b[:, 0:8],
                                          glog[0:8, i * 128:(i + 1) * 128],
                                          eye_bf[0:8, 0:8])
                PE.inc(ins)

            @blk.scalar
            def _(se, i=i):
                se.wait_ge(PE.h, PE.n)
                ins = se.copy(gtok[:, 0:8], p8b[:, 0:8])
                SE.inc(ins)

            @blk.vector
            def _(ve, i=i):
                ve.wait_ge(SE.h, SE.n)
                gt = gtok[:, 0:8]
                VE.inc(ve.reduce_max(mx8[:, 0:1], gt, axis=AX))
                ve.wait_ge(VE.h, VE.n)
                VE.inc(ve.max_index(idx8[:], mx8[:], gt))
                ve.wait_ge(VE.h, VE.n)
                VE.inc(ve.tensor_copy(idxf[:, 0:1], idx8[:, 0:1]))
                ve.wait_ge(VE.h, VE.n)
                VE.inc(ve.tensor_single_scalar(oh8[:], iota8[:], idxf[:, 0:1],
                                               ALU.is_equal))
                ve.wait_ge(VE.h, VE.n)
                VE.inc(ve.scalar_tensor_tensor(tokw[:, 0:8], oh8[:], NEG, gt,
                                               ALU.mult, ALU.add))
                ve.wait_ge(VE.h, VE.n)
                VE.inc(ve.reduce_max(mxB[:, 0:1], tokw[:, 0:8], axis=AX))
                ve.wait_ge(VE.h, VE.n)
                VE.inc(ve.max_index(idxB[:], mxB[:], tokw[:, 0:8]))
                ve.wait_ge(VE.h, VE.n)
                VE.inc(ve.tensor_copy(idxf[:, 1:2], idxB[:, 0:1]))
                ve.wait_ge(VE.h, VE.n)
                ins = ve.tensor_sub(tokw[:, 8:9], mx8[:, 0:1], mxB[:, 0:1])
                VE.inc(ins)

            @blk.scalar
            def _(se, i=i):
                se.wait_ge(VE.h, VE.n)
                se.activation(tokw[:, 9:10], tokw[:, 8:9], AF.Sigmoid)
                ins = se.activation(tokw[:, 10:11], tokw[:, 8:9], AF.Sigmoid,
                                    scale=-1.0)
                SE.inc(ins)

            @blk.vector
            def _(ve, i=i):
                ve.wait_ge(SE.h, SE.n)
                VE.inc(ve.tensor_mul(tokw[:, 16:24], oh8[:], emap_b[:]))
                ve.wait_ge(VE.h, VE.n)
                VE.inc(ve.reduce_sum(tokw[:, 11:12], tokw[:, 16:24], axis=AX))
                ve.wait_ge(VE.h, VE.n)
                VE.inc(ve.tensor_single_scalar(tokw[:, 24:32], iota8[:],
                                               idxf[:, 1:2], ALU.is_equal))
                ve.wait_ge(VE.h, VE.n)
                VE.inc(ve.tensor_mul(tokw[:, 32:40], tokw[:, 24:32],
                                     emap_b[:]))
                ve.wait_ge(VE.h, VE.n)
                VE.inc(ve.reduce_sum(tokw[:, 12:13], tokw[:, 32:40], axis=AX))
                ve.wait_ge(VE.h, VE.n)
                VE.inc(ve.tensor_single_scalar(tokw[:, 40:48], iota8[:],
                                               tokw[:, 11:12], ALU.is_equal))
                ve.wait_ge(VE.h, VE.n)
                VE.inc(ve.tensor_single_scalar(tokw[:, 48:56], iota8[:],
                                               tokw[:, 12:13], ALU.is_equal))
                ve.wait_ge(VE.h, VE.n)
                VE.inc(ve.tensor_single_scalar(tokw[:, 40:48], tokw[:, 40:48],
                                               tokw[:, 9:10], ALU.mult))
                ve.wait_ge(VE.h, VE.n)
                ins = ve.scalar_tensor_tensor(d0bf[:], tokw[:, 48:56],
                                              tokw[:, 10:11], tokw[:, 40:48],
                                              ALU.mult, ALU.add)
                VE.inc(ins)

            @blk.tensor
            def _(te, i=i):
                te.wait_ge(VE.h, VE.n)
                if i:
                    te.wait_ge(SE.h, SE.n)
                ins = nc.tensor.transpose(p8c[:], d0bf[:], eye_bf[:])
                PE.inc(ins)

            @blk.scalar
            def _(se, i=i):
                se.wait_ge(PE.h, PE.n)
                ins = se.copy(d0T[0:8, i * 128:(i + 1) * 128], p8c[0:8, :])
                SE.inc(ins)

        cx.barrier()

        for tp in range(4):
            @blk.tensor
            def _(te, tp=tp):
                if tp:
                    te.wait_ge(SE.h, SE.n)
                ins = nc.tensor.matmul(p8[0:8, 0:512], g8b[0:8, 8:16],
                                       d0T[0:8, tp * 512:(tp + 1) * 512],
                                       start=True, stop=True)
                PE.inc(ins)

            @blk.scalar
            def _(se, tp=tp):
                se.wait_ge(PE.h, PE.n)
                ins = se.copy(comb_sb[0:8, tp * 512:(tp + 1) * 512],
                              p8[0:8, 0:512])
                SE.inc(ins)

        cx.barrier()


def emit_l0ffn_experts(cx, bufs, aps, sc):
    nc, blk, s = cx.nc, cx.blk, cx.s
    GD, PE, VE, SE, GP, EV, LD = (s["GD"], s["PE"], s["VE"], s["SE"], s["GP"],
                                  s["EV"], s["LD"])
    xt, nt = bufs["xt"], bufs["nt"]
    hmid = bufs["hmid"]
    cb = sc["cb"]
    sg2 = sc["sg2"]
    tt2 = sc["tt2"]
    wsems = (s["GD0"], s["GD1"])

    with contextlib.ExitStack() as st:
        gps = [st.enter_context(nc.psum_tensor(_uq("gps"), [128, 512], F32))
               for _ in range(2)]
        ups = [st.enter_context(nc.psum_tensor(_uq("ups"), [128, 512], F32))
               for _ in range(2)]

        gp_last = {}
        for e in range(8):
            @blk.sync
            def _(q, e=e):
                if e >= 2:
                    q.wait_ge(VE.h, gp_last[e - 2])
                ins = q.dma_start(
                    out=cb[:, (e % 2) * T:(e % 2) * T + T],
                    in_=aps["comb_hbm"][e:e + 1, :].broadcast_to((128, T)))
                LD.dma(ins)
            cb_mark = LD.n

            se_marks, ve_marks, gp_marks, pe_marks = {}, {}, {}, {}
            pe_base, se_base, ve_base, gp_base = PE.n, SE.n, VE.n, GP.n

            @blk.gpsimd
            def _(gp, e=e):
                for j in range(NCH):
                    if j >= 2:
                        gp.wait_ge(PE.h, pe_base + (j - 1) * 4)
                    sl = j % 2
                    ins = gp.dma_start(
                        out=sc["w_bf"][:, sl * 2048:sl * 2048 + 1024]
                        .rearrange("p (k m) -> p k m", k=NCH),
                        in_=aps["gupT"][e][:, j * 128:(j + 1) * 128]
                        .rearrange("(k p) m -> p k m", p=128))
                    wsems[sl].dma(ins)
                    ins = gp.dma_start(
                        out=sc["w_bf"][:, sl * 2048 + 1024:sl * 2048 + 2048]
                        .rearrange("p (k m) -> p k m", k=NCH),
                        in_=aps["gupT"][e][:, 1024 + j * 128:1024 + (j + 1) * 128]
                        .rearrange("(k p) m -> p k m", p=128))
                    wsems[sl].dma(ins)
                    gp_marks[j] = wsems[sl].n

            @blk.tensor
            def _(te, e=e):
                g = 0
                for j in range(NCH):
                    sl = j % 2
                    te.wait_ge(wsems[sl].h, gp_marks[j])
                    for tp in range(4):
                        if g >= 2:
                            te.wait_ge(SE.h, se_base + g - 1)
                            te.wait_ge(VE.h, ve_base + 2 * (g - 1))
                        ps_sl = g % 2
                        for kc in range(NCH):
                            nc.tensor.matmul(
                                gps[ps_sl][:],
                                sc["w_bf"][:, sl * 2048 + kc * 128:sl * 2048 + (kc + 1) * 128],
                                nt[:, kc * T + tp * 512: kc * T + tp * 512 + 512],
                                start=(kc == 0), stop=(kc == NCH - 1))
                        ins = None
                        for kc in range(NCH):
                            ins = nc.tensor.matmul(
                                ups[ps_sl][:],
                                sc["w_bf"][:, sl * 2048 + 1024 + kc * 128:sl * 2048 + 1024 + (kc + 1) * 128],
                                nt[:, kc * T + tp * 512: kc * T + tp * 512 + 512],
                                start=(kc == 0), stop=(kc == NCH - 1))
                        pe_marks[g] = PE.inc(ins)
                        g += 1

            @blk.scalar
            def _(se, e=e):
                for g in range(32):
                    se.wait_ge(PE.h, pe_marks[g])
                    if g >= 2:
                        se.wait_ge(VE.h, ve_base + 2 * (g - 1))
                    ins = se.activation(sg2[:, (g % 2) * 512:(g % 2) * 512 + 512],
                                        gps[g % 2][:], AF.Silu)
                    se_marks[g] = SE.inc(ins)

            @blk.vector
            def _(ve, e=e):
                ve.wait_ge(LD.h, cb_mark)
                for g in range(32):
                    j, tp = g // 4, g % 4
                    ve.wait_ge(SE.h, se_marks[g])
                    VE.inc(ve.tensor_mul(
                        tt2[:, (g % 2) * 512:(g % 2) * 512 + 512],
                        sg2[:, (g % 2) * 512:(g % 2) * 512 + 512],
                        ups[g % 2][:]))
                    ve.wait_ge(VE.h, VE.n)
                    ins = ve.tensor_mul(
                        hmid[:, j * T + tp * 512: j * T + tp * 512 + 512],
                        tt2[:, (g % 2) * 512:(g % 2) * 512 + 512],
                        cb[:, (e % 2) * T + tp * 512:(e % 2) * T + tp * 512 + 512])
                    ve_marks[g] = VE.inc(ins)
            gp_last[e] = VE.n

            cx.barrier()

            def evac_res(eng, mc, tp, psap):
                sl = xt[:, mc * T + tp * 512: mc * T + tp * 512 + 512]
                return eng.tensor_add(sl, sl, psap)

            mm_stream(cx, sc, mch=NCH, kch=NCH,
                      wload=lambda mc, e=e: aps["dpT"][e][:, mc * 128:(mc + 1) * 128],
                      wkw=NCH * 128,
                      lhsT_fn=lambda mc, kc, slot: sc["w_bf"][
                          :, slot * NCH * 128 + kc * 128: slot * NCH * 128 + (kc + 1) * 128],
                      rhs_fn=lambda kc, tp: hmid[:, kc * T + tp * 512: kc * T + tp * 512 + 512],
                      evac=("vector", evac_res))
            cx.barrier()


def alloc_scratch2(st, nc):
    sc = {}

    def sb(name, shape, dt):
        sc[name] = st.enter_context(nc.sbuf_tensor(name, shape, dt))
        return sc[name]

    sb("poolf", [128, 2048], F32)
    sb("ones_col_f", [128, 1], F32)
    sb("ones_row_f", [1, 128], F32)
    sb("ones_row_bf", [1, 512], BF16)
    sb("eye_bf", [128, 128], BF16)
    sb("iota8", [128, 8], F32)
    sb("iotac", [128, 1], F32)
    sb("w_bf", [128, 2 * 2048], BF16)
    sb("gtok_all", [128, 8], F32)
    sb("mx8", [128, 8], F32)
    sb("mxB", [128, 8], F32)
    sb("idx8", [128, 8], U32)
    sb("idxB", [128, 8], U32)
    sb("idxf", [128, 2], F32)
    sb("oh8", [128, 8], BF16)
    sb("g8", [8, 64], F32)
    sb("g8b", [8, 16], BF16)
    sb("frow", [1, 16], BF16)
    sb("tokw", [128, 64], F32)
    sb("d0bf", [128, 8], BF16)
    sb("d0T", [8, 2048], BF16)
    sb("emap_b", [128, 8], F32)
    sb("comb_sb", [8, 2048], F32)
    sb("cb", [128, 2 * 2048], F32)
    sb("sg2", [128, 1024], BF16)
    sb("tt2", [128, 1024], BF16)
    sc["glog"] = sc["poolf"][0:8, 0:1024].bitcast(BF16)
    return sc


def build_launch2():
    nc = bass.Bass("TRN2", target_bir_lowering=False)
    aps = {}

    def di(name, shape):
        aps[name] = nc.dram_tensor(name, shape, F32, kind="ExternalInput").ap()

    di("x_t", [C, T])
    di("gwT", [C, 8])
    di("counts_all", [8, 8])
    di("sim", [8, 8])
    di("gupT", [8, C, 2048])
    di("dpT", [8, C, C])
    di("c_eye", [128, 128])
    di("c_iota8", [128, 8])
    di("c_iotac", [128, 1])
    aps["comb_hbm"] = nc.dram_tensor("comb_hbm", [8, T], F32,
                                     kind="Internal").ap()
    x2_t = nc.dram_tensor("x2_t", [C, T], F32, kind="ExternalOutput").ap()

    with contextlib.ExitStack() as st:
        ec = st.enter_context
        bufs = {
            "xt": ec(nc.sbuf_tensor("xt", [128, NCH * T], F32)),
            "nt": ec(nc.sbuf_tensor("nt", [128, NCH * T], BF16)),
            "hmid": ec(nc.sbuf_tensor("hmid", [128, NCH * T], BF16)),
        }
        sems = {k: Sem(ec(nc.semaphore(k)))
                for k in ("LD", "GD", "GD0", "GD1", "PE", "VE", "SE", "EV",
                          "GP", "ST")}
        sc = alloc_scratch2(st, nc)
        blk = ec(nc.Block())
        cx = Ctx(nc, blk, sems)

        @blk.sync
        def _(q):
            sems["LD"].dma(q.dma_start(
                out=bufs["xt"][:].rearrange("p (c t) -> p c t", c=NCH),
                in_=aps["x_t"].rearrange("(c p) t -> p c t", p=128)))
            sems["LD"].dma(q.dma_start(out=sc["iota8"][:], in_=aps["c_iota8"]))

        @blk.gpsimd
        def _(gp):
            sems["GD"].dma(gp.dma_start(out=sc["eye_bf"][:], in_=aps["c_eye"]))

        @blk.vector
        def _(ve):
            ve.wait_ge(sems["LD"].h, sems["LD"].n)
            ve.wait_ge(sems["GD"].h, sems["GD"].n)
            ve.memset(sc["ones_col_f"][:], 1.0)
            ve.memset(sc["ones_row_f"][:], 1.0)
            ve.memset(sc["ones_row_bf"][:], 1.0)
            ve.memset(sc["mx8"][:], NEG)
            ins = ve.memset(sc["mxB"][:], NEG)
            sems["VE"].inc(ins)

        cx.barrier()
        emit_layernorm(cx, bufs["xt"], bufs["nt"], sc)

        def evac_glog(eng, mc, tp, psap):
            return eng.copy(sc["glog"][0:8, tp * 512:(tp + 1) * 512], psap)

        mm_stream(cx, sc, mch=1, kch=NCH, mrows=8,
                  wload=lambda mc: aps["gwT"][:, :],
                  wkw=NCH * 8,
                  lhsT_fn=lambda mc, kc, slot: sc["w_bf"][
                      :, slot * NCH * 8 + kc * 8: slot * NCH * 8 + (kc + 1) * 8],
                  rhs_fn=lambda kc, tp: bufs["nt"][:, kc * T + tp * 512: kc * T + tp * 512 + 512],
                  evac=("scalar", evac_glog))
        cx.barrier()

        emit_l0_gating(cx, aps, sc)

        ST = sems["ST"]

        @blk.sync
        def _(q):
            ST.dma(q.dma_start(out=aps["comb_hbm"], in_=sc["comb_sb"][0:8, :]))
        cx.barrier()

        emit_l0ffn_experts(cx, bufs, aps, sc)

        @blk.sync
        def _(q):
            ST.dma(q.dma_start(
                out=x2_t.rearrange("(c p) t -> p c t", p=128),
                in_=bufs["xt"][:].rearrange("p (c t) -> p c t", c=NCH)))
            q.wait_ge(ST.h, ST.n)

    return nc


def launch2_inputs(inputs, b):
    gup = np.asarray(inputs["L0_ffn_gate_up_proj"], np.float32)
    dp = np.asarray(inputs["L0_ffn_down_proj"], np.float32)
    d = {
        "gwT": np.ascontiguousarray(
            np.asarray(inputs["L0_ffn_gate_weight"], np.float32).T),
        "sim": np.ascontiguousarray(
            np.asarray(inputs["L0_ffn_similarity_matrix"], np.float32)),
        "gupT": np.ascontiguousarray(gup.transpose(0, 2, 1)),
        "dpT": np.ascontiguousarray(dp.transpose(0, 2, 1)),
        "c_iotac": np.arange(128, dtype=np.float32).reshape(128, 1),
    }
    d.update(_consts())
    del d["c_triu"]
    return d


DIN = 2048
NDC = DIN // 128
NST = 16
HALF = 1024


def emit_l1attn(cx, bufs, aps, sc):
    nc, blk, s = cx.nc, cx.blk, cx.s
    LD, GD, PE, VE, SE, GP, EV, ST = (s["LD"], s["GD"], s["PE"], s["VE"],
                                      s["SE"], s["GP"], s["EV"], s["ST"])
    xt, nt = bufs["xt"], bufs["nt"]
    big = bufs["big"]
    wsems = (s["GD0"], s["GD1"])
    BB, CB = 0, 16384
    DLT, UU, DU, DA, DBU, HH, TT, ACC = (32768, 33792, 34816, 35840, 36864,
                                         37888, 38912, 39936)
    convin = sc["convin"]
    ct = sc["ct"]
    us = sc["us"]
    hcarry = sc["hcarry"]
    dtT = sc["dtT"]
    wdt_bf = sc["wdt_bf"]
    acols = sc["acols"]
    ub2 = sc["ub2"]

    emit_layernorm(cx, xt, nt, sc)

    @blk.sync
    def _(q):
        LD.dma(q.dma_start(out=sc["sp1"][:, 0:256], in_=aps["faloc"]))
        LD.dma(q.dma_start(out=sc["sp1"][:, 256:512], in_=aps["baloc"]))
        LD.dma(q.dma_start(out=sc["wconv"][:], in_=aps["wconv_c"]))
        LD.dma(q.dma_start(out=sc["dcols"][:], in_=aps["dcols_c"]))
        LD.dma(q.dma_start(out=sc["bconv"][:], in_=aps["bconv_c"]))

    @blk.gpsimd
    def _(gp):
        GD.dma(gp.dma_start(out=wdt_bf[0:64, 0:2048], in_=aps["fwdtT"]))
        GD.dma(gp.dma_start(out=wdt_bf[64:128, 0:2048], in_=aps["bwdtT"]))
        GD.dma(gp.dma_start(out=sc["bdt_bf"][:], in_=aps["bdt_rows"]))

    @blk.scalar
    def _(se):
        se.wait_ge(LD.h, LD.n)
        ins = se.activation(acols[:], sc["sp1"][:, 0:512], AF.Exp)
        SE.inc(ins)

    @blk.vector
    def _(ve):
        ve.wait_ge(SE.h, SE.n)
        ins = ve.tensor_scalar_mul(acols[:], acols[:], -1.0)
        VE.inc(ins)

    cx.barrier()

    with contextlib.ExitStack() as st:
        ps = [st.enter_context(nc.psum_tensor(_uq("wi"), [128, 512], F32))
              for _ in range(4)]
        pe_base = PE.n
        wmk, pe_marks = {}, {}

        @blk.vector
        def _(ve):
            VE.inc(ve.memset(convin[:, 0:3], 0.0))
            ve.wait_ge(VE.h, VE.n)

        @blk.gpsimd
        def _(gp):
            for mc in range(32):
                if mc >= 2:
                    gp.wait_ge(PE.h, pe_base + (mc - 1) * 4)
                sl = mc % 2
                ins = gp.dma_start(
                    out=sc["w_bf"][:, sl * 1024:(sl + 1) * 1024]
                    .rearrange("p (k m) -> p k m", k=NCH),
                    in_=aps["winT"][:, mc * 128:(mc + 1) * 128]
                    .rearrange("(k p) m -> p k m", p=128))
                wmk[mc] = wsems[sl].dma(ins)

        for mc in range(32):
            @blk.tensor
            def _(te, mc=mc):
                te.wait_ge(wsems[mc % 2].h, wmk[mc])
                if mc >= 1:
                    te.wait_ge(SE.h, SE.n)
                for tp in range(4):
                    for kc in range(NCH):
                        ins = nc.tensor.matmul(
                            ps[tp][:],
                            sc["w_bf"][:, (mc % 2) * 1024 + kc * 128:(mc % 2) * 1024 + (kc + 1) * 128],
                            nt[:, kc * T + tp * 512: kc * T + tp * 512 + 512],
                            start=(kc == 0), stop=(kc == NCH - 1))
                    pe_marks[(mc, tp)] = PE.inc(ins)
                    if tp >= 2:
                        te.wait_ge(SE.h, SE.n)

            @blk.scalar
            def _(se, mc=mc):
                se.wait_ge(ST.h, ST.n)
                se.wait_ge(VE.h, VE.n)
                for tp in range(4):
                    se.wait_ge(PE.h, pe_marks[(mc, tp)])
                    if mc < 16:
                        ins = se.copy(convin[:, 3 + tp * 512:3 + tp * 512 + 512],
                                      ps[tp][:])
                    else:
                        ins = se.activation(ct[:, tp * 512:tp * 512 + 512],
                                            ps[tp][:], AF.Silu)
                    SE.inc(ins)

            if mc < 16:
                @blk.vector
                def _(ve, mc=mc):
                    ve.wait_ge(SE.h, SE.n)
                    if mc >= 1:
                        ve.wait_ge(ST.h, ST.n)
                    wj = lambda j: sc["wconv"][:, mc * 4 + j:mc * 4 + j + 1]
                    VE.inc(ve.tensor_single_scalar(ct[:], convin[:, 0:2048],
                                                   wj(0), ALU.mult))
                    ve.wait_ge(VE.h, VE.n)
                    for j in range(1, 4):
                        VE.inc(ve.scalar_tensor_tensor(
                            ct[:], convin[:, j:j + 2048], wj(j), ct[:],
                            ALU.mult, ALU.add))
                        ve.wait_ge(VE.h, VE.n)

                @blk.scalar
                def _(se, mc=mc):
                    se.wait_ge(VE.h, VE.n)
                    ins = se.activation(us[:], ct[:], AF.Silu,
                                        bias=sc["bconv"][:, mc:mc + 1])
                    SE.inc(ins)

                @blk.sync
                def _(q, mc=mc):
                    q.wait_ge(SE.h, SE.n)
                    ST.dma(q.dma_start(
                        out=aps["u_hbm"][mc * 128:(mc + 1) * 128, :],
                        in_=us[:]))
            else:
                @blk.sync
                def _(q, mc=mc):
                    q.wait_ge(SE.h, SE.n)
                    ST.dma(q.dma_start(
                        out=aps["sres_hbm"][(mc - 16) * 128:(mc - 15) * 128, :],
                        in_=ct[:]))

        cx.barrier()

    for dr in range(2):
        rev = dr == 1
        wxT = aps["fwxT"] if dr == 0 else aps["bwxT"]

        with contextlib.ExitStack() as st:
            dps = st.enter_context(nc.psum_tensor(_uq("dbc"), [128, 512], F32))
            gd_marks = {}

            @blk.gpsimd
            def _(gp):
                for kc in range(NDC):
                    if kc >= 2:
                        gp.wait_ge(PE.h, PE.n + 0)
                    sl = kc % 2
                    ins = gp.dma_start(
                        out=ub2[:, sl * 2048:(sl + 1) * 2048],
                        in_=aps["u_hbm"][kc * 128:(kc + 1) * 128, :])
                    gd_marks[kc] = wsems[sl].dma(ins)

            @blk.sync
            def _(q):
                LD.dma(q.dma_start(
                    out=sc["wx_bf"][:].rearrange("p (k m) -> p k m", k=NDC),
                    in_=wxT.rearrange("(k p) m -> p k m", p=128)))

            raise NotImplementedError


DIN = 2048
NDC = DIN // 128
NST = 16
HALF = 1024


def emit_l1attn(cx, bufs, aps, sc):
    nc, blk, s = cx.nc, cx.blk, cx.s
    LD, GD, PE, VE, SE, GP, EV, ST = (s["LD"], s["GD"], s["PE"], s["VE"],
                                      s["SE"], s["GP"], s["EV"], s["ST"])
    big = bufs["big"]
    nt = bufs["nt"]
    wsems = (s["GD0"], s["GD1"])
    xt = big[:, 0:16384]
    acols = sc["acols"]
    hcarry = sc["hcarry"]

    @blk.sync
    def _(q):
        LD.dma(q.dma_start(
            out=xt.rearrange("p (c t) -> p c t", c=NCH),
            in_=aps["x_t"].rearrange("(c p) t -> p c t", p=128)))
        LD.dma(q.dma_start(out=big[:, 18432:18944], in_=aps["aloc2"]))
        LD.dma(q.dma_start(out=sc["wconv"][:], in_=aps["wconv_c"]))
        LD.dma(q.dma_start(out=sc["dcols"][:], in_=aps["dcols_c"]))
        LD.dma(q.dma_start(out=sc["bconv"][:], in_=aps["bconv_c"]))
        LD.dma(q.dma_start(out=sc["bdtc"][:], in_=aps["bdt_c"]))

    @blk.vector
    def _(ve):
        ve.wait_ge(LD.h, LD.n)
        ve.memset(sc["ones_col_f"][:], 1.0)
        ve.memset(sc["ones_row_f"][:], 1.0)
        ins = ve.memset(sc["ones_row_bf"][:], 1.0)
        VE.inc(ins)

    @blk.scalar
    def _(se):
        se.wait_ge(LD.h, LD.n)
        ins = se.activation(acols[:], big[:, 18432:18944], AF.Exp)
        SE.inc(ins)

    @blk.vector
    def _(ve):
        ve.wait_ge(SE.h, SE.n)
        ins = ve.tensor_scalar_mul(acols[:], acols[:], -1.0)
        VE.inc(ins)

    cx.barrier()
    emit_layernorm(cx, xt, nt, sc)

    @blk.sync
    def _(q):
        ST.dma(q.dma_start(out=aps["xspill"], in_=xt))
    cx.barrier()

    convin = big[:, 32768:34822]
    ct = big[:, 34824:36872]
    us = big[:, 36872:38920]
    with contextlib.ExitStack() as st:
        ps = [st.enter_context(nc.psum_tensor(_uq("wi"), [128, 512], F32))
              for _ in range(4)]
        pe_base = PE.n
        wmk, pe_marks = {}, {}

        @blk.vector
        def _(ve):
            VE.inc(ve.memset(convin[:, 0:3], 0.0))
            ve.wait_ge(VE.h, VE.n)

        @blk.gpsimd
        def _(gp):
            for mc in range(32):
                if mc >= 2:
                    gp.wait_ge(PE.h, pe_base + (mc - 1) * 4)
                sl = mc % 2
                ins = gp.dma_start(
                    out=sc["w_bf"][:, sl * 1024:(sl + 1) * 1024]
                    .rearrange("p (k m) -> p k m", k=NCH),
                    in_=aps["winT"][:, mc * 128:(mc + 1) * 128]
                    .rearrange("(k p) m -> p k m", p=128))
                wmk[mc] = wsems[sl].dma(ins)

        for mc in range(32):
            @blk.tensor
            def _(te, mc=mc):
                te.wait_ge(wsems[mc % 2].h, wmk[mc])
                if mc >= 1:
                    te.wait_ge(SE.h, SE.n)
                for tp in range(4):
                    for kc in range(NCH):
                        ins = nc.tensor.matmul(
                            ps[tp][:],
                            sc["w_bf"][:, (mc % 2) * 1024 + kc * 128:(mc % 2) * 1024 + (kc + 1) * 128],
                            nt[:, kc * T + tp * 512: kc * T + tp * 512 + 512],
                            start=(kc == 0), stop=(kc == NCH - 1))
                    pe_marks[(mc, tp)] = PE.inc(ins)

            @blk.scalar
            def _(se, mc=mc):
                se.wait_ge(ST.h, ST.n)
                se.wait_ge(VE.h, VE.n)
                for tp in range(4):
                    se.wait_ge(PE.h, pe_marks[(mc, tp)])
                    if mc < 16:
                        ins = se.copy(convin[:, 3 + tp * 512:3 + tp * 512 + 512],
                                      ps[tp][:])
                    else:
                        ins = se.activation(ct[:, tp * 512:tp * 512 + 512],
                                            ps[tp][:], AF.Silu)
                    SE.inc(ins)

            if mc < 16:
                @blk.vector
                def _(ve, mc=mc):
                    ve.wait_ge(SE.h, SE.n)
                    ve.wait_ge(ST.h, ST.n)
                    wj = lambda j: sc["wconv"][:, mc * 4 + j:mc * 4 + j + 1]
                    VE.inc(ve.tensor_single_scalar(ct[:, 0:2048],
                                                   convin[:, 0:2048],
                                                   wj(0), ALU.mult))
                    ve.wait_ge(VE.h, VE.n)
                    for j in range(1, 4):
                        VE.inc(ve.scalar_tensor_tensor(
                            ct[:, 0:2048], convin[:, j:j + 2048], wj(j),
                            ct[:, 0:2048], ALU.mult, ALU.add))
                        ve.wait_ge(VE.h, VE.n)

                @blk.scalar
                def _(se, mc=mc):
                    se.wait_ge(VE.h, VE.n)
                    ins = se.activation(us[:, 0:2048], ct[:, 0:2048], AF.Silu,
                                        bias=sc["bconv"][:, mc:mc + 1])
                    SE.inc(ins)

                @blk.sync
                def _(q, mc=mc):
                    q.wait_ge(SE.h, SE.n)
                    ST.dma(q.dma_start(
                        out=aps["u_hbm"][mc * 128:(mc + 1) * 128, :],
                        in_=us[:, 0:2048]))
            else:
                @blk.sync
                def _(q, mc=mc):
                    q.wait_ge(SE.h, SE.n)
                    ST.dma(q.dma_start(
                        out=aps["sres_hbm"][(mc - 16) * 128:(mc - 15) * 128, :],
                        in_=ct[:, 0:2048]))
        cx.barrier()

    for dr in range(2):
        wxT = aps["fwxT"] if dr == 0 else aps["bwxT"]
        ub2 = big[:, 32768:34816].bitcast(BF16)
        wx_bf = big[:, 34816:35584].bitcast(BF16)
        dtT = big[0:64, 36608:37632].bitcast(BF16)
        bcst = big[64:96, 37632:39680]

        with contextlib.ExitStack() as st:
            dps = [st.enter_context(nc.psum_tensor(_uq("dbc"), [96, 512], F32))
                   for _ in range(4)]
            pe_base = PE.n
            gd_marks = {}

            @blk.gpsimd
            def _(gp):
                ins = gp.dma_start(
                    out=wx_bf[:, 0:NDC * 96].rearrange("p (k m) -> p k m",
                                                       k=NDC),
                    in_=wxT.rearrange("(k p) m -> p k m", p=128))
                GD.dma(ins)
                for kc in range(NDC):
                    if kc >= 2:
                        gp.wait_ge(PE.h, pe_base + kc - 1)
                    sl = kc % 2
                    ins = gp.dma_start(
                        out=ub2[:, sl * 2048:(sl + 1) * 2048],
                        in_=aps["u_hbm"][kc * 128:(kc + 1) * 128, :])
                    gd_marks[kc] = wsems[sl].dma(ins)

            @blk.tensor
            def _(te):
                te.wait_ge(GD.h, GD.n)
                for kc in range(NDC):
                    te.wait_ge(wsems[kc % 2].h, gd_marks[kc])
                    ins = None
                    for tp in range(4):
                        ins = nc.tensor.matmul(
                            dps[tp][0:96, :],
                            wx_bf[:, kc * 96:(kc + 1) * 96],
                            ub2[:, (kc % 2) * 2048 + tp * 512:(kc % 2) * 2048 + tp * 512 + 512],
                            start=(kc == 0), stop=(kc == NDC - 1))
                    PE.inc(ins)

            @blk.scalar
            def _(se):
                se.wait_ge(PE.h, PE.n)
                for tp in range(4):
                    se.copy(dtT[0:64, tp * 512:(tp + 1) * 512],
                            dps[tp][0:64, :])
                    ins = se.copy(bcst[:, tp * 512:(tp + 1) * 512],
                                  dps[tp][64:96, :])
                    SE.inc(ins)

            @blk.sync
            def _(q, dr=dr):
                q.wait_ge(SE.h, SE.n)
                ST.dma(q.dma_start(out=aps["bc_hbm"][dr * 32:dr * 32 + 32, :],
                                   in_=bcst[:]))
            cx.barrier()

        dstage = big[:, 32768:34816]
        sp1 = big[:, 34816:35328]
        sp2 = big[:, 35328:35840]
        sp3 = big[:, 35840:36352]
        wdt = big[0:64, 38656:40704].bitcast(BF16)

        @blk.gpsimd
        def _(gp, dr=dr):
            src = aps["fwdtT"] if dr == 0 else aps["bwdtT"]
            GD.dma(gp.dma_start(out=wdt[:, 0:2048], in_=src))
        cx.barrier()

        with contextlib.ExitStack() as st:
            ps4 = [st.enter_context(nc.psum_tensor(_uq("dl"), [128, 512], F32))
                   for _ in range(4)]
            for mc in range(NDC):
                pe_marks = {}

                @blk.tensor
                def _(te, mc=mc, dr=dr):
                    if mc >= 1:
                        te.wait_ge(VE.h, VE.n)
                    for tp in range(4):
                        ins = nc.tensor.matmul(
                            ps4[tp][:],
                            wdt[:, mc * 128:(mc + 1) * 128],
                            dtT[0:64, tp * 512:(tp + 1) * 512],
                            start=True, stop=True)
                        pe_marks[tp] = PE.inc(ins)

                for tp in range(4):
                    @blk.scalar
                    def _(se, mc=mc, tp=tp, dr=dr):
                        se.wait_ge(PE.h, pe_marks[tp])
                        se.wait_ge(VE.h, VE.n)
                        bc = sc["bdtc"][:, dr * 16 + mc:dr * 16 + mc + 1]
                        SE.inc(se.activation(sp1[:], ps4[tp][:], AF.Abs,
                                             bias=bc))
                        se.wait_ge(SE.h, SE.n)
                        SE.inc(se.activation(sp1[:], sp1[:], AF.Exp,
                                             scale=-1.0))
                        se.wait_ge(SE.h, SE.n)
                        ins = se.activation(sp2[:], sp1[:], AF.Ln, bias=1.0)
                        SE.inc(ins)

                    @blk.vector
                    def _(ve, mc=mc, tp=tp, dr=dr):
                        ve.wait_ge(PE.h, pe_marks[tp])
                        ve.wait_ge(VE.h, VE.n)
                        if tp == 0:
                            ve.wait_ge(ST.h, ST.n)
                        bc = sc["bdtc"][:, dr * 16 + mc:dr * 16 + mc + 1]
                        VE.inc(ve.tensor_scalar(sp3[:], ps4[tp][:], bc,
                                                0.0, ALU.add, ALU.max))
                        ve.wait_ge(SE.h, SE.n)
                        ve.wait_ge(VE.h, VE.n)
                        ins = ve.tensor_add(dstage[:, tp * 512:(tp + 1) * 512],
                                            sp3[:], sp2[:])
                        VE.inc(ins)

                @blk.sync
                def _(q, mc=mc):
                    q.wait_ge(VE.h, VE.n)
                    ST.dma(q.dma_start(
                        out=aps["delta_hbm"][mc * 128:(mc + 1) * 128, :],
                        in_=dstage[:, 0:2048]))
            cx.barrier()

        BBo, CBo = 0, 16384
        DLT, UU, DU, DA, DBU, HH, TT2, ACC = (32768, 33792, 34816, 35840,
                                              36864, 37888, 38912, 39936)
        halves = (0, 1) if dr == 0 else (1, 0)
        for hi, ha in enumerate(halves):
            t0 = ha * HALF

            @blk.sync
            def _(q, dr=dr, t0=t0):
                for n in range(NST):
                    LD.dma(q.dma_start(
                        out=big[:, BBo + n * HALF: BBo + (n + 1) * HALF],
                        in_=aps["bc_hbm"][dr * 32 + n:dr * 32 + n + 1,
                                          t0:t0 + HALF]
                        .broadcast_to((128, HALF))))
                    LD.dma(q.dma_start(
                        out=big[:, CBo + n * HALF: CBo + (n + 1) * HALF],
                        in_=aps["bc_hbm"][dr * 32 + 16 + n:dr * 32 + 17 + n,
                                          t0:t0 + HALF]
                        .broadcast_to((128, HALF))))
            cx.barrier()

            for tile in range(NDC):
                @blk.sync
                def _(q, tile=tile, t0=t0):
                    q.wait_ge(SE.h, SE.n)
                    q.wait_ge(VE.h, VE.n)
                    q.wait_ge(GP.h, GP.n)
                    LD.dma(q.dma_start(
                        out=big[:, DLT:DLT + HALF],
                        in_=aps["delta_hbm"][tile * 128:(tile + 1) * 128,
                                             t0:t0 + HALF]))
                    LD.dma(q.dma_start(
                        out=big[:, UU:UU + HALF],
                        in_=aps["u_hbm"][tile * 128:(tile + 1) * 128,
                                         t0:t0 + HALF]))
                ld_mark = LD.n

                @blk.vector
                def _(ve, tile=tile, dr=dr):
                    ve.wait_ge(LD.h, ld_mark)
                    ve.wait_ge(ST.h, ST.n)
                    VE.inc(ve.tensor_mul(big[:, DU:DU + HALF],
                                         big[:, DLT:DLT + HALF],
                                         big[:, UU:UU + HALF]))
                    ve.wait_ge(VE.h, VE.n)
                    ins = ve.tensor_single_scalar(
                        big[:, ACC:ACC + HALF], big[:, UU:UU + HALF],
                        sc["dcols"][:, dr * 16 + tile:dr * 16 + tile + 1],
                        ALU.mult)
                    VE.inc(ins)

                for n in range(NST):
                    @blk.scalar
                    def _(se, tile=tile, n=n, dr=dr):
                        se.wait_ge(VE.h, VE.n)
                        ins = se.activation(
                            big[:, DA:DA + HALF], big[:, DLT:DLT + HALF],
                            AF.Exp,
                            scale=acols[:, dr * 256 + tile * 16 + n:
                                        dr * 256 + tile * 16 + n + 1])
                        SE.inc(ins)

                    @blk.gpsimd
                    def _(gp, n=n):
                        gp.wait_ge(VE.h, VE.n)
                        ins = gp.tensor_mul(
                            big[:, DBU:DBU + HALF], big[:, DU:DU + HALF],
                            big[:, BBo + n * HALF:BBo + (n + 1) * HALF])
                        GP.inc(ins)

                    @blk.vector
                    def _(ve, tile=tile, n=n, hi=hi, dr=dr):
                        ve.wait_ge(SE.h, SE.n)
                        ve.wait_ge(GP.h, GP.n)
                        col = tile * 16 + n
                        if dr == 0:
                            da = big[:, DA:DA + HALF]
                            db = big[:, DBU:DBU + HALF]
                            hh = big[:, HH:HH + HALF]
                            carry_col = HH + HALF - 1
                        else:
                            da = big[:, DA:DA + HALF][:, ::-1]
                            db = big[:, DBU:DBU + HALF][:, ::-1]
                            hh = big[:, HH:HH + HALF][:, ::-1]
                            carry_col = HH
                        init = 0.0 if hi == 0 else hcarry[:, col:col + 1]
                        VE.inc(ve.tensor_tensor_scan(hh, da, db, init,
                                                     ALU.mult, ALU.add))
                        ve.wait_ge(VE.h, VE.n)
                        ins = ve.tensor_copy(hcarry[:, col:col + 1],
                                             big[:, carry_col:carry_col + 1])
                        VE.inc(ins)

                    @blk.gpsimd
                    def _(gp, n=n):
                        gp.wait_ge(VE.h, VE.n)
                        ins = gp.tensor_mul(
                            big[:, TT2:TT2 + HALF], big[:, HH:HH + HALF],
                            big[:, CBo + n * HALF:CBo + (n + 1) * HALF])
                        GP.inc(ins)

                    @blk.vector
                    def _(ve, n=n):
                        ve.wait_ge(GP.h, GP.n)
                        ins = ve.tensor_add(big[:, ACC:ACC + HALF],
                                            big[:, ACC:ACC + HALF],
                                            big[:, TT2:TT2 + HALF])
                        VE.inc(ins)
                        ve.wait_ge(VE.h, VE.n)

                if dr == 0:
                    @blk.sync
                    def _(q, tile=tile, t0=t0):
                        q.wait_ge(VE.h, VE.n)
                        ST.dma(q.dma_start(
                            out=aps["accf_hbm"][tile * 128:(tile + 1) * 128,
                                                t0:t0 + HALF],
                            in_=big[:, ACC:ACC + HALF]))
                else:
                    @blk.sync
                    def _(q, tile=tile, t0=t0):
                        q.wait_ge(VE.h, VE.n)
                        LD.dma(q.dma_start(
                            out=big[:, DA:DA + HALF],
                            in_=aps["accf_hbm"][tile * 128:(tile + 1) * 128,
                                                t0:t0 + HALF]))
                        LD.dma(q.dma_start(
                            out=big[:, DBU:DBU + HALF],
                            in_=aps["sres_hbm"][tile * 128:(tile + 1) * 128,
                                                t0:t0 + HALF]))
                    ld2 = LD.n

                    @blk.vector
                    def _(ve, tile=tile, t0=t0, ld2=ld2):
                        ve.wait_ge(LD.h, ld2)
                        VE.inc(ve.tensor_add(big[:, ACC:ACC + HALF],
                                             big[:, ACC:ACC + HALF],
                                             big[:, DA:DA + HALF]))
                        ve.wait_ge(VE.h, VE.n)
                        VE.inc(ve.tensor_scalar_mul(big[:, ACC:ACC + HALF],
                                                    big[:, ACC:ACC + HALF],
                                                    0.5))
                        ve.wait_ge(VE.h, VE.n)
                        ins = ve.tensor_mul(
                            big[:, DU:DU + HALF].bitcast(BF16)[:, 0:HALF],
                            big[:, ACC:ACC + HALF], big[:, DBU:DBU + HALF])
                        VE.inc(ins)

                    @blk.sync
                    def _(q, tile=tile, t0=t0):
                        q.wait_ge(VE.h, VE.n)
                        ST.dma(q.dma_start(
                            out=aps["y_hbm"][tile * 128:(tile + 1) * 128,
                                             t0:t0 + HALF],
                            in_=big[:, DU:DU + HALF].bitcast(BF16)[:, 0:HALF]))
            cx.barrier()

    ybig = big[:, 0:16384].bitcast(BF16)
    xt2 = big[:, 16384:32768]

    @blk.sync
    def _(q):
        LD.dma(q.dma_start(
            out=ybig.rearrange("p (k t) -> p k t", k=NDC),
            in_=aps["y_hbm"].rearrange("(k p) t -> p k t", p=128)))
        LD.dma(q.dma_start(
            out=xt2.rearrange("p (c t) -> p c t", c=NCH),
            in_=aps["xspill"].rearrange("p (c t) -> p c t", c=NCH)))
    cx.barrier()

    def evac_res(eng, mc, tp, psap):
        sl = xt2[:, mc * T + tp * 512: mc * T + tp * 512 + 512]
        return eng.tensor_add(sl, sl, psap)

    mm_stream(cx, sc, mch=NCH, kch=NDC,
              wload=lambda mc: aps["woutT"][:, mc * 128:(mc + 1) * 128],
              wkw=NDC * 128,
              lhsT_fn=lambda mc, kc, slot: sc["w_bf"][
                  :, slot * NDC * 128 + kc * 128: slot * NDC * 128 + (kc + 1) * 128],
              rhs_fn=lambda kc, tp: ybig[:, kc * T + tp * 512: kc * T + tp * 512 + 512],
              evac=("vector", evac_res))
    cx.barrier()


def alloc_scratch3(st, nc):
    sc = {}

    def sb(name, shape, dt):
        sc[name] = st.enter_context(nc.sbuf_tensor(name, shape, dt))
        return sc[name]

    sb("ones_col_f", [128, 1], F32)
    sb("ones_row_f", [1, 128], F32)
    sb("ones_row_bf", [1, 512], BF16)
    sb("w_bf", [128, 4096], BF16)
    sb("acols", [128, 512], F32)
    sb("hcarry", [128, 256], F32)
    sb("wconv", [128, 64], F32)
    sb("dcols", [128, 32], F32)
    sb("bconv", [128, 16], F32)
    sb("bdtc", [128, 32], F32)
    return sc


def build_launch3():
    nc = bass.Bass("TRN2", target_bir_lowering=False)
    aps = {}

    def di(name, shape):
        aps[name] = nc.dram_tensor(name, shape, F32, kind="ExternalInput").ap()

    di("x_t", [C, T])
    di("winT", [C, 2 * DIN])
    di("wconv_c", [128, 64])
    di("bconv_c", [128, 16])
    di("woutT", [DIN, C])
    di("fwxT", [DIN, 96])
    di("bwxT", [DIN, 96])
    di("fwdtT", [64, DIN])
    di("bwdtT", [64, DIN])
    di("bdt_c", [128, 32])
    di("aloc2", [128, 512])
    di("dcols_c", [128, 32])

    def scr(name, shape, dt=F32):
        aps[name] = nc.dram_tensor(name, shape, dt, kind="Internal").ap()

    scr("u_hbm", [DIN, T])
    scr("sres_hbm", [DIN, T])
    scr("delta_hbm", [DIN, T])
    scr("accf_hbm", [DIN, T])
    scr("y_hbm", [DIN, T], BF16)
    scr("bc_hbm", [64, T])
    scr("xspill", [128, NCH * T])
    x3_t = nc.dram_tensor("x3_t", [C, T], F32, kind="ExternalOutput").ap()

    with contextlib.ExitStack() as st:
        ec = st.enter_context
        bufs = {
            "big": ec(nc.sbuf_tensor("big", [128, 40960], F32)),
            "nt": ec(nc.sbuf_tensor("nt", [128, NCH * T], BF16)),
        }
        sems = {k: Sem(ec(nc.semaphore(k)))
                for k in ("LD", "GD", "GD0", "GD1", "PE", "VE", "SE", "EV",
                          "GP", "ST")}
        sc = alloc_scratch3(st, nc)
        sc["poolf"] = bufs["big"][:, 16384:18432]
        blk = ec(nc.Block())
        cx = Ctx(nc, blk, sems)

        emit_l1attn(cx, bufs, aps, sc)

        ST = sems["ST"]

        @blk.sync
        def _(q):
            ST.dma(q.dma_start(
                out=x3_t.rearrange("(c p) t -> p c t", p=128),
                in_=bufs["big"][:, 16384:32768]
                .rearrange("p (c t) -> p c t", c=NCH)))
            q.wait_ge(ST.h, ST.n)

    return nc


def launch3_inputs(inputs, b):
    wconv = np.asarray(inputs["L1_attn_W_conv"], np.float32)
    wcc = np.zeros((128, 64), np.float32)
    for mc in range(16):
        for j in range(4):
            wcc[:, mc * 4 + j] = wconv[mc * 128:(mc + 1) * 128, 0, j]
    bconv = _cols128(inputs["L1_attn_b_conv"])
    fa = np.asarray(inputs["L1_attn_fA_log"], np.float32)
    ba = np.asarray(inputs["L1_attn_bA_log"], np.float32)
    aloc = np.zeros((128, 512), np.float32)
    for dc in range(16):
        aloc[:, dc * 16:(dc + 1) * 16] = fa[dc * 128:(dc + 1) * 128, :]
        aloc[:, 256 + dc * 16:256 + (dc + 1) * 16] = ba[dc * 128:(dc + 1) * 128, :]
    dcols = np.concatenate([_cols128(inputs["L1_attn_fD"]),
                            _cols128(inputs["L1_attn_bD"])], axis=1)
    bdtc = np.concatenate([_cols128(inputs["L1_attn_fbdt"]),
                           _cols128(inputs["L1_attn_bbdt"])], axis=1)
    return {
        "winT": np.ascontiguousarray(
            np.asarray(inputs["L1_attn_W_in"], np.float32).T),
        "wconv_c": wcc,
        "bconv_c": bconv,
        "woutT": np.ascontiguousarray(
            np.asarray(inputs["L1_attn_W_out"], np.float32).T),
        "fwxT": np.ascontiguousarray(
            np.asarray(inputs["L1_attn_fWx"], np.float32).T),
        "bwxT": np.ascontiguousarray(
            np.asarray(inputs["L1_attn_bWx"], np.float32).T),
        "fwdtT": np.ascontiguousarray(
            np.asarray(inputs["L1_attn_fWdt"], np.float32).T),
        "bwdtT": np.ascontiguousarray(
            np.asarray(inputs["L1_attn_bWdt"], np.float32).T),
        "bdt_c": bdtc,
        "aloc2": np.ascontiguousarray(aloc),
        "dcols_c": np.ascontiguousarray(dcols),
    }


def emit_l1ffn_gating(cx, aps, sc):
    nc, blk, s = cx.nc, cx.blk, cx.s
    PE, VE, SE = s["PE"], s["VE"], s["SE"]
    glog = sc["glog"]
    gtok = sc["gtok_all"]
    eye_bf = sc["eye_bf"]
    iota8 = sc["iota8"]
    mx8, idx8, idxf, oh8 = sc["mx8"], sc["idx8"], sc["idxf"], sc["oh8"]
    mxB, idxB = sc["mxB"], sc["idxB"]
    tokw = sc["tokw"]
    maskT = sc["maskT"]
    wm8 = sc["wm8"]
    ohb = sc["ohb"]

    with contextlib.ExitStack() as st:
        p8b = st.enter_context(nc.psum_tensor(_uq("p8b"), [128, 8], BF16))
        p8c = st.enter_context(nc.psum_tensor(_uq("p8c"), [8, 128], BF16))

        for i in range(16):
            @blk.tensor
            def _(te, i=i):
                te.wait_ge(SE.h, SE.n)
                te.wait_ge(VE.h, VE.n)
                ins = nc.tensor.transpose(p8b[:, 0:8],
                                          glog[0:8, i * 128:(i + 1) * 128],
                                          eye_bf[0:8, 0:8])
                PE.inc(ins)

            @blk.scalar
            def _(se, i=i):
                se.wait_ge(PE.h, PE.n)
                ins = se.copy(gtok[:, 0:8], p8b[:, 0:8])
                SE.inc(ins)

            @blk.vector
            def _(ve, i=i):
                ve.wait_ge(SE.h, SE.n)
                gt = gtok[:, 0:8]
                VE.inc(ve.reduce_max(mx8[:, 0:1], gt, axis=AX))
                ve.wait_ge(VE.h, VE.n)
                VE.inc(ve.max_index(idx8[:], mx8[:], gt))
                ve.wait_ge(VE.h, VE.n)
                VE.inc(ve.tensor_copy(idxf[:, 0:1], idx8[:, 0:1]))
                ve.wait_ge(VE.h, VE.n)
                VE.inc(ve.tensor_single_scalar(oh8[:], iota8[:], idxf[:, 0:1],
                                               ALU.is_equal))
                ve.wait_ge(VE.h, VE.n)
                VE.inc(ve.scalar_tensor_tensor(tokw[:, 0:8], oh8[:], NEG, gt,
                                               ALU.mult, ALU.add))
                ve.wait_ge(VE.h, VE.n)
                VE.inc(ve.reduce_max(mxB[:, 0:1], tokw[:, 0:8], axis=AX))
                ve.wait_ge(VE.h, VE.n)
                VE.inc(ve.max_index(idxB[:], mxB[:], tokw[:, 0:8]))
                ve.wait_ge(VE.h, VE.n)
                VE.inc(ve.tensor_copy(idxf[:, 1:2], idxB[:, 0:1]))
                ve.wait_ge(VE.h, VE.n)
                VE.inc(ve.tensor_max(tokw[:, 8:9], idxf[:, 0:1], idxf[:, 1:2]))
                ve.wait_ge(VE.h, VE.n)
                VE.inc(ve.tensor_tensor(tokw[:, 9:10], idxf[:, 0:1],
                                        tokw[:, 8:9], ALU.is_equal))
                VE.inc(ve.tensor_sub(tokw[:, 10:11], mx8[:, 0:1], mxB[:, 0:1]))
                ve.wait_ge(VE.h, VE.n)
                VE.inc(ve.tensor_mul(tokw[:, 11:12], tokw[:, 9:10],
                                     tokw[:, 10:11]))
                ve.wait_ge(VE.h, VE.n)
                VE.inc(ve.tensor_add(tokw[:, 12:13], tokw[:, 11:12],
                                     mxB[:, 0:1]))
                VE.inc(ve.tensor_scalar_mul(tokw[:, 13:14], mx8[:, 0:1], -1.0))
                ve.wait_ge(VE.h, VE.n)
                VE.inc(ve.engine_nop())

            @blk.scalar
            def _(se, i=i):
                se.wait_ge(VE.h, VE.n)
                se.activation(tokw[:, 16:24], gtok[:, 0:8], AF.Exp,
                              bias=tokw[:, 13:14])
                ins = se.activation(tokw[:, 14:15], tokw[:, 12:13], AF.Exp,
                                    bias=tokw[:, 13:14])
                SE.inc(ins)

            @blk.vector
            def _(ve, i=i):
                ve.wait_ge(SE.h, SE.n)
                VE.inc(ve.reduce_sum(tokw[:, 15:16], tokw[:, 16:24], axis=AX))
                ve.wait_ge(VE.h, VE.n)
                VE.inc(ve.reciprocal(tokw[:, 15:16], tokw[:, 15:16]))
                ve.wait_ge(VE.h, VE.n)
                VE.inc(ve.tensor_mul(tokw[:, 14:15], tokw[:, 14:15],
                                     tokw[:, 15:16]))
                ve.wait_ge(VE.h, VE.n)
                VE.inc(ve.tensor_single_scalar(ohb[:], iota8[:], tokw[:, 8:9],
                                               ALU.is_equal))
                ve.wait_ge(VE.h, VE.n)
                ins = ve.tensor_single_scalar(wm8[:], ohb[:], tokw[:, 14:15],
                                              ALU.mult)
                VE.inc(ins)

            @blk.tensor
            def _(te, i=i):
                te.wait_ge(VE.h, VE.n)
                if i:
                    te.wait_ge(SE.h, SE.n)
                ins = nc.tensor.transpose(p8c[:], ohb[:], eye_bf[:])
                PE.inc(ins)

            @blk.scalar
            def _(se, i=i):
                se.wait_ge(PE.h, PE.n)
                ins = se.copy(maskT[0:8, i * 128:(i + 1) * 128], p8c[0:8, :])
                SE.inc(ins)

            @blk.vector
            def _(ve, i=i):
                ve.wait_ge(VE.h, VE.n)
                ins = ve.tensor_copy(sc["wmbf"][:, i * 8:(i + 1) * 8], wm8[:])
                VE.inc(ins)

        cx.barrier()

        for i in range(16):
            @blk.tensor
            def _(te, i=i):
                if i:
                    te.wait_ge(SE.h, SE.n)
                ins = nc.tensor.transpose(p8c[:],
                                          sc["wmbf"][:, i * 8:(i + 1) * 8],
                                          eye_bf[:])
                PE.inc(ins)

            @blk.scalar
            def _(se, i=i):
                se.wait_ge(PE.h, PE.n)
                ins = se.copy(sc["wmT"][0:8, i * 128:(i + 1) * 128],
                              p8c[0:8, :])
                SE.inc(ins)
        cx.barrier()


def emit_l1ffn_experts(cx, bufs, aps, sc):
    nc, blk, s = cx.nc, cx.blk, cx.s
    GD, PE, VE, SE, GP, EV, LD, ST = (s["GD"], s["PE"], s["VE"], s["SE"],
                                      s["GP"], s["EV"], s["LD"], s["ST"])
    xt, nt = bufs["xt"], bufs["nt"]
    hm = bufs["hm"]
    hmid = hm[:].bitcast(BF16)
    xin = sc["xin"]
    wm_b = sc["wm_b"]
    maskT = sc["maskT"]
    relub1 = sc["relub1"]
    call_c = sc["call_c"]
    wsems = (s["GD0"], s["GD1"])

    @blk.vector
    def _(ve):
        ins = ve.tensor_relu(relub1[:], sc["b1c"][:])
        VE.inc(ins)
    cx.barrier()

    with contextlib.ExitStack() as st:
        mps = [st.enter_context(nc.psum_tensor(_uq("l4"), [128, 512], F32))
               for _ in range(2)]
        c_ps = st.enter_context(nc.psum_tensor(_uq("cps"), [128, 8], F32))

        for e in range(8):
            @blk.gpsimd
            def _(gp, e=e):
                if e >= 1:
                    gp.wait_ge(VE.h, VE.n)
                ins = gp.dma_start(
                    out=wm_b[:],
                    in_=aps["wm_hbm"][e:e + 1, :].broadcast_to((128, T)))
                GD.dma(ins)
            wm_mark = GD.n

            for th in range(2):
                h0 = th * 1024

                @blk.vector
                def _(ve, e=e, th=th, h0=h0):
                    ve.wait_ge(GD.h, wm_mark)
                    if th == 0 and e >= 1:
                        pass
                    ve.wait_ge(PE.h, PE.n)
                    ins = None
                    for c in range(NCH):
                        ins = ve.tensor_mul(
                            xin[:, c * 1024:(c + 1) * 1024],
                            nt[:, c * T + h0: c * T + h0 + 1024],
                            wm_b[:, h0:h0 + 1024])
                    VE.inc(ins)

                pe_base, ev_base = PE.n, EV.n
                wmk = {}
                w1_marks = {}

                @blk.gpsimd
                def _(gp, e=e, th=th, pe_entry=PE.n):
                    for mc in range(32):
                        if mc >= 2:
                            gp.wait_ge(PE.h, pe_base + (mc - 1) * 2)
                        else:
                            gp.wait_ge(PE.h, pe_entry)
                        sl = mc % 2
                        ins = gp.dma_start(
                            out=sc["w_bf"][:, sl * 1024:(sl + 1) * 1024]
                            .rearrange("p (k m) -> p k m", k=NCH),
                            in_=aps["w1T"][e][:, mc * 128:(mc + 1) * 128]
                            .rearrange("(k p) m -> p k m", p=128))
                        wmk[mc] = wsems[sl].dma(ins)

                @blk.tensor
                def _(te, e=e, th=th):
                    te.wait_ge(VE.h, VE.n)
                    te.wait_ge(EV.h, ev_base)
                    g = 0
                    for mc in range(32):
                        te.wait_ge(wsems[mc % 2].h, wmk[mc])
                        for tp in range(2):
                            if g >= 2:
                                te.wait_ge(EV.h, ev_base + g - 1)
                            ins = None
                            for kc in range(NCH):
                                ins = nc.tensor.matmul(
                                    mps[g % 2][:],
                                    sc["w_bf"][:, (mc % 2) * 1024 + kc * 128:(mc % 2) * 1024 + (kc + 1) * 128],
                                    xin[:, kc * 1024 + tp * 512: kc * 1024 + tp * 512 + 512],
                                    start=(kc == 0), stop=(kc == NCH - 1))
                            w1_marks[g] = PE.inc(ins)
                            g += 1

                @blk.scalar
                def _(se, e=e, th=th):
                    for g in range(64):
                        mc, tp = g // 2, g % 2
                        se.wait_ge(PE.h, w1_marks[g])
                        ins = se.activation(
                            hmid[:, mc * 1024 + tp * 512: mc * 1024 + tp * 512 + 512],
                            mps[g % 2][:], AF.Relu,
                            bias=sc["b1c"][:, e * 32 + mc:e * 32 + mc + 1])
                        EV.inc(ins)

                pe_base2, ev_base2 = PE.n, EV.n
                wmk2 = {}

                @blk.gpsimd
                def _(gp, e=e, th=th, pe_entry=PE.n):
                    for mc in range(NCH):
                        if mc >= 2:
                            gp.wait_ge(PE.h, pe_base2 + (mc - 1) * (3 if th == 0 else 2))
                        else:
                            gp.wait_ge(PE.h, pe_entry)
                        sl = mc % 2
                        ins = gp.dma_start(
                            out=sc["w_bf"][:, sl * 4096:(sl + 1) * 4096]
                            .rearrange("p (k m) -> p k m", k=32),
                            in_=aps["w2T"][e][:, mc * 128:(mc + 1) * 128]
                            .rearrange("(k p) m -> p k m", p=128))
                        wmk2[mc] = wsems[sl].dma(ins)

                w2_marks = {}

                @blk.tensor
                def _(te, e=e, th=th, h0=h0):
                    te.wait_ge(EV.h, ev_base2)
                    g = 0
                    for mc in range(NCH):
                        te.wait_ge(wsems[mc % 2].h, wmk2[mc])
                        for tp in range(2):
                            if g >= 2:
                                te.wait_ge(EV.h, ev_base2 + g - 1)
                            ins = None
                            for kc in range(32):
                                ins = nc.tensor.matmul(
                                    mps[g % 2][:],
                                    sc["w_bf"][:, (mc % 2) * 4096 + kc * 128:(mc % 2) * 4096 + (kc + 1) * 128],
                                    hmid[:, kc * 1024 + tp * 512: kc * 1024 + tp * 512 + 512],
                                    start=(kc == 0), stop=(kc == 31))
                            w2_marks[g] = PE.inc(ins)
                            g += 1

                        if th == 0:
                            ins = None
                            for kc in range(32):
                                ins = nc.tensor.matmul(
                                    c_ps[:, mc:mc + 1],
                                    sc["w_bf"][:, (mc % 2) * 4096 + kc * 128:(mc % 2) * 4096 + (kc + 1) * 128],
                                    relub1[:, e * 32 + kc:e * 32 + kc + 1],
                                    start=(kc == 0), stop=(kc == 31))
                            PE.inc(ins)

                @blk.vector
                def _(ve, e=e, th=th, h0=h0):
                    for g in range(16):
                        mc, tp = g // 2, g % 2
                        ve.wait_ge(PE.h, w2_marks[g])
                        sl2 = xt[:, mc * T + h0 + tp * 512:
                                 mc * T + h0 + tp * 512 + 512]
                        ins = ve.tensor_add(sl2, sl2, mps[g % 2][:])
                        EV.inc(ins)

            @blk.scalar
            def _(se, e=e):
                se.wait_ge(PE.h, PE.n)
                ins = se.copy(call_c[:, e * 8:(e + 1) * 8], c_ps[:])
                SE.inc(ins)
            cx.barrier()

        bct = sc["bct"]
        nsum = sc["nsum"]

        @blk.vector
        def _(ve):
            VE.inc(ve.tensor_add(call_c[:], call_c[:], sc["b2c"][:]))
            ve.wait_ge(VE.h, VE.n)

        p8c2 = st.enter_context(nc.psum_tensor(_uq("p8c2"), [8, 128], BF16))
        p1c = st.enter_context(nc.psum_tensor(_uq("p1c"), [1, 128], BF16))

        for mc in range(NCH):
            @blk.vector
            def _(ve, mc=mc):
                ve.wait_ge(VE.h, VE.n)
                ve.wait_ge(SE.h, SE.n)
                VE.inc(ve.tensor_copy(sc["bcc"][:, 0:8],
                                      call_c[:, mc:64:NCH]))
                ve.wait_ge(VE.h, VE.n)
                VE.inc(ve.tensor_sub(sc["bcc"][:, 8:16],
                                     call_c[:, mc:64:NCH],
                                     sc["b2c"][:, mc:64:NCH]))
                ve.wait_ge(VE.h, VE.n)
                VE.inc(ve.reduce_sum(sc["wm8"][:, 0:1], sc["bcc"][:, 8:16],
                                     axis=AX))
                ve.wait_ge(VE.h, VE.n)
                ins = ve.tensor_scalar_mul(sc["bcc"][:, 17:18],
                                           sc["wm8"][:, 0:1], -1.0)
                VE.inc(ins)

            @blk.tensor
            def _(te, mc=mc):
                te.wait_ge(VE.h, VE.n)
                if mc:
                    te.wait_ge(SE.h, SE.n)
                nc.tensor.transpose(p8c2[:], sc["bcc"][:, 0:8], sc["eye_bf"][:])
                ins = nc.tensor.transpose(p1c[:], sc["bcc"][:, 17:18],
                                          sc["eye_bf"][:])
                PE.inc(ins)

            @blk.scalar
            def _(se, mc=mc):
                se.wait_ge(PE.h, PE.n)
                se.copy(bct[0:8, mc * 128:(mc + 1) * 128], p8c2[0:8, :])
                ins = se.copy(nsum[0:1, mc * 128:(mc + 1) * 128], p1c[0:1, :])
                SE.inc(ins)
        cx.barrier()

        for mc in range(NCH):
            for tp in range(4):
                @blk.tensor
                def _(te, mc=mc, tp=tp):
                    g = mc * 4 + tp
                    if g >= 2:
                        te.wait_ge(EV.h, EV.n - 1)
                    nc.tensor.matmul(mps[g % 2][:],
                                     bct[0:8, mc * 128:(mc + 1) * 128],
                                     maskT[0:8, tp * 512:(tp + 1) * 512],
                                     start=True, stop=False)
                    ins = nc.tensor.matmul(mps[g % 2][:],
                                           nsum[0:1, mc * 128:(mc + 1) * 128],
                                           sc["ones_row_bf"][:, 0:512],
                                           start=False, stop=True)
                    PE.inc(ins)

                @blk.vector
                def _(ve, mc=mc, tp=tp):
                    ve.wait_ge(PE.h, PE.n)
                    g = mc * 4 + tp
                    sl2 = xt[:, mc * T + tp * 512: mc * T + tp * 512 + 512]
                    ins = ve.tensor_add(sl2, sl2, mps[g % 2][:])
                    EV.inc(ins)
        cx.barrier()


def alloc_scratch4(st, nc):
    sc = {}

    def sb(name, shape, dt):
        sc[name] = st.enter_context(nc.sbuf_tensor(name, shape, dt))
        return sc[name]

    sb("ones_col_f", [128, 1], F32)
    sb("ones_row_f", [1, 128], F32)
    sb("ones_row_bf", [1, 512], BF16)
    sb("eye_bf", [128, 128], BF16)
    sb("iota8", [128, 8], F32)
    sb("w_bf", [128, 8192], BF16)
    sb("gtok_all", [128, 8], F32)
    sb("mx8", [128, 8], F32)
    sb("mxB", [128, 8], F32)
    sb("idx8", [128, 8], U32)
    sb("idxB", [128, 8], U32)
    sb("idxf", [128, 2], F32)
    sb("oh8", [128, 8], BF16)
    sb("ohb", [128, 8], BF16)
    sb("wm8", [128, 8], F32)
    sb("wmbf", [128, 128], BF16)
    sb("tokw", [128, 24], F32)
    sb("maskT", [8, 2048], BF16)
    sb("xin", [128, 8192], BF16)
    sb("wm_b", [128, 2048], BF16)
    sb("relub1", [128, 256], BF16)
    sb("b1c", [128, 256], F32)
    sb("b2c", [128, 64], F32)
    sb("call_c", [128, 64], F32)
    sb("bcc", [128, 18], BF16)
    sb("gbc", [128, 1], F32)
    return sc


def build_launch4():
    nc = bass.Bass("TRN2", target_bir_lowering=False)
    aps = {}

    def di(name, shape):
        aps[name] = nc.dram_tensor(name, shape, F32, kind="ExternalInput").ap()

    di("x_t", [C, T])
    di("gwT", [C, 8])
    di("gb_c", [128, 1])
    di("w1T", [8, C, 4096])
    di("w2T", [8, 4096, C])
    di("b1_c", [128, 256])
    di("b2_c", [128, 64])
    di("c_eye", [128, 128])
    di("c_iota8", [128, 8])
    aps["wm_hbm"] = nc.dram_tensor("wm_hbm", [8, T], F32, kind="Internal").ap()
    x4_t = nc.dram_tensor("x4_t", [C, T], F32, kind="ExternalOutput").ap()

    with contextlib.ExitStack() as st:
        ec = st.enter_context
        bufs = {
            "xt": ec(nc.sbuf_tensor("xt", [128, NCH * T], F32)),
            "nt": ec(nc.sbuf_tensor("nt", [128, NCH * T], BF16)),
            "hm": ec(nc.sbuf_tensor("hm", [128, 16384], F32)),
        }
        sems = {k: Sem(ec(nc.semaphore(k)))
                for k in ("LD", "GD", "GD0", "GD1", "PE", "VE", "SE", "EV",
                          "GP", "ST")}
        sc = alloc_scratch4(st, nc)
        sc["poolf"] = bufs["hm"][:, 0:2048]
        sc["glog"] = bufs["hm"][0:8, 2048:3072].bitcast(BF16)
        sc["wmT"] = bufs["hm"][0:8, 3072:5120]
        sc["bct"] = bufs["hm"][0:8, 0:512].bitcast(BF16)
        sc["nsum"] = bufs["hm"][0:1, 512:1024].bitcast(BF16)
        blk = ec(nc.Block())
        cx = Ctx(nc, blk, sems)

        @blk.sync
        def _(q):
            sems["LD"].dma(q.dma_start(
                out=bufs["xt"][:].rearrange("p (c t) -> p c t", c=NCH),
                in_=aps["x_t"].rearrange("(c p) t -> p c t", p=128)))
            sems["LD"].dma(q.dma_start(out=sc["iota8"][:], in_=aps["c_iota8"]))
            sems["LD"].dma(q.dma_start(out=sc["b1c"][:], in_=aps["b1_c"]))
            sems["LD"].dma(q.dma_start(out=sc["b2c"][:], in_=aps["b2_c"]))
            sems["LD"].dma(q.dma_start(out=sc["gbc"][:], in_=aps["gb_c"]))

        @blk.gpsimd
        def _(gp):
            sems["GD"].dma(gp.dma_start(out=sc["eye_bf"][:], in_=aps["c_eye"]))

        @blk.vector
        def _(ve):
            ve.wait_ge(sems["LD"].h, sems["LD"].n)
            ve.wait_ge(sems["GD"].h, sems["GD"].n)
            ve.memset(sc["ones_col_f"][:], 1.0)
            ve.memset(sc["ones_row_f"][:], 1.0)
            ve.memset(sc["ones_row_bf"][:], 1.0)
            ve.memset(sc["mx8"][:], NEG)
            ins = ve.memset(sc["mxB"][:], NEG)
            sems["VE"].inc(ins)

        cx.barrier()
        emit_layernorm(cx, bufs["xt"], bufs["nt"], sc)

        def evac_glog(eng, mc, tp, psap):
            return eng.activation(sc["glog"][0:8, tp * 512:(tp + 1) * 512],
                                  psap, AF.Identity, bias=sc["gbc"][0:8, 0:1])

        mm_stream(cx, sc, mch=1, kch=NCH, mrows=8,
                  wload=lambda mc: aps["gwT"][:, :],
                  wkw=NCH * 8,
                  lhsT_fn=lambda mc, kc, slot: sc["w_bf"][
                      :, slot * NCH * 8 + kc * 8: slot * NCH * 8 + (kc + 1) * 8],
                  rhs_fn=lambda kc, tp: bufs["nt"][:, kc * T + tp * 512: kc * T + tp * 512 + 512],
                  evac=("scalar", evac_glog))
        cx.barrier()

        emit_l1ffn_gating(cx, aps, sc)

        @blk.sync
        def _(q):
            sems["ST"].dma(q.dma_start(out=aps["wm_hbm"], in_=sc["wmT"][0:8, :]))
        cx.barrier()

        emit_l1ffn_experts(cx, bufs, aps, sc)

        @blk.sync
        def _(q):
            sems["ST"].dma(q.dma_start(
                out=x4_t.rearrange("(c p) t -> p c t", p=128),
                in_=bufs["xt"][:].rearrange("p (c t) -> p c t", c=NCH)))
            q.wait_ge(sems["ST"].h, sems["ST"].n)

    return nc


def launch4_inputs(inputs, b):
    w1 = np.asarray(inputs["L1_ffn_w1"], np.float32)
    w2 = np.asarray(inputs["L1_ffn_w2"], np.float32)
    b1 = np.asarray(inputs["L1_ffn_b1"], np.float32)
    b2 = np.asarray(inputs["L1_ffn_b2"], np.float32)
    b1c = np.zeros((128, 256), np.float32)
    for e in range(8):
        b1c[:, e * 32:(e + 1) * 32] = b1[e].reshape(32, 128).T
    b2c = np.zeros((128, 64), np.float32)
    for e in range(8):
        for mc in range(8):
            b2c[:, e * 8 + mc] = b2[e, mc * 128:(mc + 1) * 128]
    gb = np.zeros((128, 1), np.float32)
    gb[0:8, 0] = np.asarray(inputs["L1_ffn_gate_b"], np.float32)
    d = {
        "gwT": np.ascontiguousarray(
            np.asarray(inputs["L1_ffn_gate_w"], np.float32).T),
        "gb_c": gb,
        "w1T": np.ascontiguousarray(w1.transpose(0, 2, 1)),
        "w2T": np.ascontiguousarray(w2.transpose(0, 2, 1)),
        "b1_c": b1c,
        "b2_c": b2c,
    }
    d.update(_consts())
    del d["c_triu"]
    return d


def kernel(**inputs):
    cores = list(range(NCORES))

    nc1 = build_launch1()
    maps1 = [launch1_inputs(inputs, b) for b in range(NCORES)]
    r1 = run_bass_kernel_spmd(nc1, maps1, core_ids=cores).results

    counts_all = np.ascontiguousarray(
        np.concatenate([r1[b]["cnt"] for b in range(NCORES)], axis=1))

    nc2 = build_launch2()
    base2 = launch2_inputs(inputs, 0)
    maps2 = []
    for b in range(NCORES):
        d = dict(base2)
        d["x_t"] = np.ascontiguousarray(r1[b]["x1_t"])
        d["counts_all"] = counts_all
        maps2.append(d)
    r2 = run_bass_kernel_spmd(nc2, maps2, core_ids=cores).results

    nc3 = build_launch3()
    base3 = launch3_inputs(inputs, 0)
    maps3 = []
    for b in range(NCORES):
        d = dict(base3)
        d["x_t"] = np.ascontiguousarray(r2[b]["x2_t"])
        maps3.append(d)
    r3 = run_bass_kernel_spmd(nc3, maps3, core_ids=cores).results

    nc4 = build_launch4()
    base4 = launch4_inputs(inputs, 0)
    maps4 = []
    for b in range(NCORES):
        d = dict(base4)
        d["x_t"] = np.ascontiguousarray(r3[b]["x3_t"])
        maps4.append(d)
    r4 = run_bass_kernel_spmd(nc4, maps4, core_ids=cores).results

    out = np.stack([r4[b]["x4_t"].T for b in range(NCORES)], axis=0)
    return np.ascontiguousarray(out.astype(np.float32))
